# version 07
# baseline (speedup 1.0000x reference)
import math

import numpy as np
import ml_dtypes

import concourse.bass as bass
import concourse.mybir as mybir
from concourse.bass_utils import run_bass_kernel_spmd

F32 = mybir.dt.float32
BF16 = mybir.dt.bfloat16
AF = mybir.ActivationFunctionType
ALU = mybir.AluOpType
AX = mybir.AxisListType

D = 2048
DC = 16
KEYS = 4096
TOK = 2048
NQT = 16
NKB = 16
KB = 256
CKV = 512
CC = 4
NH = 16
NPAIR = 8
ROPE = 64
VD = 64
FF = 1408
NF = 11
NPASS = 10
SCL = 1.0 / math.sqrt(192.0)
EPS = 1e-5
BASE = 10000.0

ENGINES = ("tensor", "vector", "scalar", "sync", "gpsimd")


def _bf(x):
    return np.ascontiguousarray(np.asarray(x).astype(ml_dtypes.bfloat16))


def _f32(x):
    return np.ascontiguousarray(np.asarray(x, dtype=np.float32))


def _lhsT_chunks(WT, m):
    K = WT.shape[0]
    return _f32(WT.reshape(K // 128, 128, m).transpose(1, 0, 2).reshape(128, (K // 128) * m))


def prep_shared(inp):
    d = {}
    Wdkv = _f32(inp["L0_attn_W_down_kv"])
    Wdq = _f32(inp["L0_attn_W_down_q"])
    Wkr = _f32(inp["L0_attn_W_kr"])
    Wupk = _f32(inp["L0_attn_W_up_k"])
    Wupv = _f32(inp["L0_attn_W_up_v"])
    Wupq = _f32(inp["L0_attn_W_up_q"])
    Wqr = _f32(inp["L0_attn_W_qr"])
    Wfc = _f32(inp["L0_attn_W_fc"])

    d["wA_dkv"] = _lhsT_chunks(Wdkv.T, 512)
    d["wA_dq"] = _lhsT_chunks(Wdq.T, 512)
    d["wA_kr"] = _lhsT_chunks(Wkr.T, 64)
    d["wA_upk"] = _lhsT_chunks(Wupk.T, 1024)
    d["wA_upv"] = _lhsT_chunks(Wupv.T, 1024)
    qcqr = np.empty((CKV, NH * 128), dtype=np.float32)
    for h in range(NH):
        qcqr[:, h * 128:h * 128 + 64] = Wupq.T[:, h * 64:(h + 1) * 64]
        qcqr[:, h * 128 + 64:h * 128 + 128] = Wqr.T[:, h * 64:(h + 1) * 64]
    d["wA_qcqr"] = _lhsT_chunks(qcqr, 2048)
    d["wA_fc"] = _lhsT_chunks(Wfc.T, 2048)

    d["b_dkv"] = _f32(np.asarray(inp["L0_attn_b_down_kv"]).reshape(4, 128).T)
    d["b_dq"] = _f32(np.asarray(inp["L0_attn_b_down_q"]).reshape(4, 128).T)
    d["b_kr"] = _f32(np.asarray(inp["L0_attn_b_kr"]).reshape(64, 1))
    d["b_upk"] = _f32(np.tile(np.asarray(inp["L0_attn_b_up_k"]).reshape(16, 64).T, (2, 1)))
    bq = np.asarray(inp["L0_attn_b_up_q"]).reshape(16, 64)
    br = np.asarray(inp["L0_attn_b_qr"]).reshape(16, 64)
    d["b_qcqr"] = _f32(np.concatenate([bq.T, br.T], axis=0))
    d["b_upv"] = _f32(np.asarray(inp["L0_attn_b_up_v"]).reshape(1, 1024))
    d["b_fc"] = _f32(np.asarray(inp["L0_attn_b_fc"]).reshape(1, 2048))

    i2 = np.arange(0, ROPE, 2, dtype=np.float64)
    theta = 1.0 / BASE ** (i2 / ROPE)
    ang = np.arange(KEYS, dtype=np.float64)[:, None] * theta[None, :]
    d["_sinr"] = _f32(np.repeat(np.sin(ang), 2, axis=1).T)
    d["_cosr"] = _f32(np.repeat(np.cos(ang), 2, axis=1).T)
    d["Tk_sin"] = d["_sinr"]
    d["Tk_cos"] = d["_cosr"]

    Rr = np.zeros((64, 64), dtype=np.float32)
    for i in range(32):
        Rr[2 * i, 2 * i + 1] = -1.0
        Rr[2 * i + 1, 2 * i] = 1.0
    Rq = np.zeros((128, 128), dtype=np.float32)
    Rq[64:, 64:] = Rr
    d["RqT"] = _bf(Rq.T)
    d["R64T"] = _bf(Rr.T)
    d["I128"] = _bf(np.eye(128))
    d["I128f"] = _f32(np.eye(128))

    d["wR"] = _lhsT_chunks(_f32(inp["L0_ffn_w_router"]), 8)

    w1 = _f32(inp["L0_ffn_w1"])
    w2 = _f32(inp["L0_ffn_w2"])
    w3 = _f32(inp["L0_ffn_w3"])
    ws1 = _f32(inp["L0_ffn_ws1"])
    ws2 = _f32(inp["L0_ffn_ws2"])
    ws3 = _f32(inp["L0_ffn_ws3"])
    up_all = np.concatenate([w1, ws1.reshape(D, 2, FF).transpose(1, 0, 2)], axis=0)
    gate_all = np.concatenate([w3, ws3.reshape(D, 2, FF).transpose(1, 0, 2)], axis=0)
    down_all = np.concatenate([w2, ws2.reshape(2, FF, D)], axis=0)

    def w13(a):
        return _f32(a.reshape(NPASS, DC, 128, NF, 128).transpose(0, 3, 2, 1, 4)
                    .reshape(NPASS, NF, 128, 2048))

    d["W1L"] = w13(up_all)
    d["W3L"] = w13(gate_all)
    d["W2L"] = _f32(down_all.reshape(NPASS, NF, 128, 2048).transpose(0, 2, 1, 3)
                    .reshape(NPASS, 128, NF * 2048))
    return d


def prep_core(inp, shared, core):
    b, half = core // 2, core % 2
    h = _f32(inp["L0_attn_h"][b])
    d = {k: v for k, v in shared.items() if not k.startswith("_")}
    d["h_kb"] = _f32(h.reshape(NKB, KB, DC, 128).transpose(0, 3, 2, 1).reshape(NKB, 128, DC * KB))
    ho = h[half * TOK:(half + 1) * TOK]
    d["h_q_kb"] = _f32(ho.reshape(8, KB, DC, 128).transpose(0, 3, 2, 1).reshape(8, 128, DC * KB))
    d["h_own"] = _f32(ho.reshape(NQT, 128, D))
    tq_s = np.zeros((128, TOK), dtype=np.float32)
    tq_c = np.zeros((128, TOK), dtype=np.float32)
    tq_s[64:] = shared["_sinr"][:, half * TOK:(half + 1) * TOK]
    tq_c[64:] = shared["_cosr"][:, half * TOK:(half + 1) * TOK]
    d["Tq_sin"] = _f32(tq_s)
    d["Tq_cos"] = _f32(tq_c)
    return d


INPUT_SPECS = {
    "h_kb": (F32, [NKB, 128, DC * KB]),
    "h_q_kb": (F32, [8, 128, DC * KB]),
    "h_own": (F32, [NQT, 128, D]),
    "wA_dkv": (F32, [128, 8192]), "wA_dq": (F32, [128, 8192]), "wA_kr": (F32, [128, 1024]),
    "wA_upk": (F32, [128, 4096]), "wA_upv": (F32, [128, 4096]), "wA_qcqr": (F32, [128, 8192]),
    "wA_fc": (F32, [128, 16384]),
    "b_dkv": (F32, [128, 4]), "b_dq": (F32, [128, 4]), "b_kr": (F32, [64, 1]),
    "b_upk": (F32, [128, 16]), "b_qcqr": (F32, [128, 16]), "b_upv": (F32, [1, 1024]),
    "b_fc": (F32, [1, 2048]),
    "Tk_sin": (F32, [64, 4096]), "Tk_cos": (F32, [64, 4096]),
    "Tq_sin": (F32, [128, 2048]), "Tq_cos": (F32, [128, 2048]),
    "RqT": (BF16, [128, 128]), "R64T": (BF16, [64, 64]),
    "I128": (BF16, [128, 128]), "I128f": (F32, [128, 128]),
    "wR": (F32, [128, 128]),
    "W1L": (F32, [NPASS, NF, 128, 2048]), "W3L": (F32, [NPASS, NF, 128, 2048]),
    "W2L": (F32, [NPASS, 128, NF * 2048]),
}


class Build:
    def __init__(self, nc):
        self.nc = nc
        self.q = {e: [] for e in ENGINES}
        self.cnt = {}
        self.sems = {}
        self.semnames = []

    def sem(self, name):
        if name not in self.cnt:
            self.cnt[name] = 0
            self.semnames.append(name)
        return name

    def emit(self, eng, fn):
        self.q[eng].append(fn)

    def wait(self, eng, sem, val):
        if val is None or val <= 0:
            return
        self.sem(sem)
        self.emit(eng, lambda e, s=sem, v=val: e.wait_ge(self.sems[s], v))

    def op(self, eng, fn, inc=None, n=1):
        if inc is None:
            self.emit(eng, fn)
            return None
        self.sem(inc)
        self.cnt[inc] += n
        val = self.cnt[inc]
        self.emit(eng, lambda e, f=fn, s=inc, k=n: f(e).then_inc(self.sems[s], k))
        return val

    def finalize(self):
        nc = self.nc
        import contextlib
        with contextlib.ExitStack() as stack:
            for name in self.semnames:
                self.sems[name] = stack.enter_context(nc.semaphore(name))
            with nc.Block() as block:
                for eng in ENGINES:
                    getattr(block, eng)(lambda e, q=self.q[eng]: [f(e) for f in q])
        return nc


class Ring:

    def __init__(self, B, slots, producer="tensor"):
        self.B = B
        self.slots = slots
        self.free_at = [("v", 0)] * len(slots)
        self.i = 0
        self.producer = producer

    def next(self):
        idx = self.i % len(self.slots)
        self.i += 1
        sem, val = self.free_at[idx]
        self.B.wait(self.producer, sem, val)
        return idx

    def ap(self, idx):
        s = self.slots[idx]
        return s() if callable(s) else s

    def drained(self, idx, sem, val):
        self.free_at[idx] = (sem, val)


A3_SUB = 9
NPASS_RUN = NPASS


def build_program(debug=(), phases=6):
    nc = bass.Bass("TRN2", target_bir_lowering=False)
    B = Build(nc)

    inp = {}
    for name, (dt, shape) in INPUT_SPECS.items():
        inp[name] = nc.dram_tensor(name, shape, dt, kind="ExternalInput").ap()

    out_hbm = nc.dram_tensor("out", [TOK, D], F32, kind="ExternalOutput").ap()
    x1_hbm = nc.dram_tensor("x1_hbm", [TOK, D], F32,
                            kind="ExternalOutput" if "x1" in debug else "Internal").ap()
    y_hbm = nc.dram_tensor("y_hbm", [TOK, D], F32,
                           kind="ExternalOutput" if "y" in debug else "Internal").ap()

    def dbg(name, shape, dt=F32):
        return nc.dram_tensor("dbg_" + name, shape, dt, kind="ExternalOutput").ap() \
            if name in debug else None

    dbg_ckv = dbg("ckv", [128, CC * KEYS])
    dbg_cq = dbg("cq", [128, CC * TOK])
    dbg_krr = dbg("krr", [128, KEYS])
    dbg_K = dbg("K", [2, 128, KEYS])
    dbg_V = dbg("V", [128, 32 * 130])
    dbg_Q = dbg("Q", [2, 128, TOK])
    dbg_OT = dbg("OT", [128, 8 * TOK])
    dbg_cmb = dbg("cmb", [128, 128])
    dbg_z = dbg("z", [128, 2048])
    dbg_bfcB = dbg_z and nc.dram_tensor("dbg_bfcB", [128, 2048], F32, kind="ExternalOutput").ap()
    dbg_stats = dbg_z and nc.dram_tensor("dbg_stats", [128, 16], F32, kind="ExternalOutput").ap()
    dbg_h1s = dbg("h1s", [128, NF * 2048])

    A = nc.alloc_sbuf_tensor
    I128 = A("sb_I128", [128, 128], BF16)
    I128f = A("sb_I128f", [128, 128], F32)
    RqT = A("sb_RqT", [128, 128], BF16)
    R64T = A("sb_R64T", [64, 64], BF16)
    ones1f = A("ones1f", [1, 128], F32)
    b_dkv = A("sb_b_dkv", [128, 4], F32)
    b_dq = A("sb_b_dq", [128, 4], F32)
    b_kr = A("sb_b_kr", [64, 1], F32)
    b_upk = A("sb_b_upk", [128, 16], F32)
    b_qcqr = A("sb_b_qcqr", [128, 16], F32)
    smll = A("smll", [128, 16], F32)
    cmb_all = A("cmb_all", [128, 128], F32)
    eps_col = A("eps_col", [128, 1], F32)

    ARB1 = A("ARB1", [128, 24576], BF16)
    ARB2 = A("ARB2", [128, 55296], BF16)
    ARF = A("ARF", [128, 8192], F32)
    ARF2 = A("ARF2", [128, 4096], F32)

    ckv_sb = ARB1[:, 0:16384]
    cq_sb = ARB1[:, 16384:24576]
    h1s_sb = ARB1[:, 0:22528]
    ARB1f = ARB1.bitcast(F32)
    hob = [ARB1f[:, 0:2048], ARB1f[:, 2048:4096]]
    bfcB = ARB1f[:, 4096:6144]
    b_fc_sb = ARB1f[0:1, 6144:8192]
    outb = [ARB1f[:, 0:2048], ARB1f[:, 2048:4096]]

    S3, S4, S5, S6 = 0, 16384, 22528, 30720
    S7, S8 = 34944, 51328
    wA_dkv_sb = ARB2[:, S3:S3 + 8192]
    wA_dq_sb = ARB2[:, S3 + 8192:S3 + 16384]
    wA_kr_sb = ARB2[:, S4:S4 + 1024]
    krw_sb = ARB2[:, S4 + 1024:S4 + 5120]
    wA_upk_sb = ARB2[:, S3:S3 + 4096]
    wA_upv_sb = ARB2[:, S3 + 4096:S3 + 8192]
    wA_qcqr_sb = ARB2[:, S3 + 8192:S3 + 16384]
    wA_fc_sb = ARB2[:, S3:S3 + 16384]
    Qh = [ARB2[:, S4:S4 + 2048], ARB2[:, S4 + 2048:S4 + 4096]]
    Pr = ARB2[:, S4 + 4096:S4 + 6144]
    Kb = [ARB2[:, S5:S5 + 4096], ARB2[:, S5 + 4096:S5 + 8192]]
    Vp = ARB2[:, S6:S6 + 4224]
    OT_sb = ARB2[:, S7:S7 + 16384]
    W2s_sb = ARB2[:, S3:S3 + 22528]
    xT_sb = ARB2[:, S5:S5 + 32768]
    x1bf_sb = ARB2[:, S4:S4 + 2048]
    x1bf_b = ARB2[:, S4 + 2048:S4 + 4096]
    x1bf_c = ARB2[:, S4 + 4096:S4 + 6144]

    hb = [ARF[:, 0:2048].bitcast(BF16), ARF[:, 2048:4096].bitcast(BF16)]
    Tq_sin = ARF[:, 0:2048]
    Tq_cos = ARF[:, 2048:4096]
    Tk_sin = ARF[0:64, 4096:6144]
    Tk_cos = ARF[0:64, 6144:8192]
    krsc = [ARF[0:64, 0:512], ARF[0:64, 512:1024]]
    lb_sb = ARF2[0:64, 0:512]
    lrow = [ARF2[0:1, 2048:2560], ARF2[0:1, 2560:3072]]
    lrow_ln = [ARF2[0:1, 3072:3584], ARF2[0:1, 3584:4096]]
    rsc = ARF2[:, 512:1024]
    bvt = ARF2[:, 1024:1152]
    b_upv_sb = ARF[0:1, 1024:2048]
    z_sb = ARF[:, 0:2048]
    x1f = [ARF[:, 2048:4096], ARF[:, 4096:6144]]
    sigT = ARF[0:8, 0:2048]
    stok = ARF[:, 2048:2176]
    cmbt = ARF[:, 2176:2304]
    top8 = ARF[:, 2304:2312]
    cmb_sc = ARF[:, 2312:2440]
    ystage = [ARF[:, i * 1024:(i + 1) * 1024] for i in range(2)]
    tsil = ARB1[:, 22528:23552].bitcast(F32)
    wR_sb = ARB1[:, 23552:23680]
    w1ring = [ARF2[:, 0:1024].bitcast(BF16), ARF2[:, 1024:2048].bitcast(BF16)]
    w3ring = [ARF2[:, 2048:3072].bitcast(BF16), ARF2[:, 3072:4096].bitcast(BF16)]
    x1b = [ARF[:, 0:2048], ARF[:, 2048:4096]]
    yb = [ARF[:, 4096:6144], ARF[:, 6144:8192]]

    PA = nc.alloc_psum_tensor("PA", [128, 1024], F32)
    PB = nc.alloc_psum_tensor("PB", [128, 1024], F32)
    PC = nc.alloc_psum_tensor("PC", [128, 1024], F32)
    PE = nc.alloc_psum_tensor("PE", [128, 512], F32)
    PF = nc.alloc_psum_tensor("PF", [128, 512], F32)
    PEb = PE.bitcast(BF16)

    def vop(fn):
        vv = B.op("vector", fn, inc="v")
        B.wait("vector", "v", vv)
        return vv

    def g_dma(dst, src, accum=False):
        if accum:
            return B.op("gpsimd", lambda e, d=dst, s=src:
                        e.dma_start(out=d, in_=s, accum_op=ALU.add), inc="g", n=16)
        return B.op("gpsimd", lambda e, d=dst, s=src: e.dma_start(out=d, in_=s), inc="g", n=16)

    def s_dma(dst, src):
        return B.op("sync", lambda e, d=dst, s=src: e.dma_start(out=d, in_=s), inc="hw", n=16)

    B.op("vector", lambda e: e.memset(ones1f[:], 1.0), inc="v")
    B.op("vector", lambda e: e.memset(eps_col[:], EPS), inc="v")
    v_vones = B.op("vector", lambda e: e.memset(Vp, 1.0), inc="v")
    hw0 = 0
    for dst, src in [(I128[:], inp["I128"]), (I128f[:], inp["I128f"]), (RqT[:], inp["RqT"]),
                     (R64T[:], inp["R64T"]), (b_dkv[:], inp["b_dkv"]), (b_dq[:], inp["b_dq"]),
                     (b_kr[:], inp["b_kr"]), (b_upk[:], inp["b_upk"]),
                     (b_qcqr[:], inp["b_qcqr"])]:
        hw0 = s_dma(dst, src)
    g_w1 = 0
    for dst, src in [(wA_dkv_sb, inp["wA_dkv"]), (wA_dq_sb, inp["wA_dq"]),
                     (wA_kr_sb, inp["wA_kr"])]:
        g_w1 = g_dma(dst, src)

    a1_slots = [lambda: PA[:, 0:256], lambda: PA[:, 512:768],
                lambda: PB[:, 0:256], lambda: PB[:, 512:768],
                lambda: PC[:, 0:256], lambda: PC[:, 512:768]]
    a1r = Ring(B, a1_slots)

    hwh = [g_dma(hb[i], inp["h_kb"][i]) for i in range(2)]
    t_hb = [0, 0]

    def a1_group(rhs_sb, w_sb, w_stride, m_off, mpart, bias, dst):
        si = a1r.next()
        ps = a1r.ap(si)
        tv = None
        for c in range(DC):
            def mm(e, ps=ps, c=c, rhs=rhs_sb, w=w_sb, ws=w_stride, mo=m_off, mp=mpart,
                   first=(c == 0), last=(c == DC - 1)):
                return e.matmul(ps[0:mp, :], w[:, c * ws + mo:c * ws + mo + mp],
                                rhs[:, c * 256:(c + 1) * 256], start=first, stop=last)
            tv = B.op("tensor", mm, inc="t" if c == DC - 1 else None)
        if si % 2 == 0:
            B.wait("vector", "t", tv)
            vv = B.op("vector", lambda e, ps=ps, mp=mpart, b=bias, d=dst:
                      e.tensor_scalar(d, ps[0:mp, :], b, None, ALU.add), inc="v")
            a1r.drained(si, "v", vv)
        else:
            B.wait("scalar", "t", tv)
            vv = B.op("scalar", lambda e, ps=ps, mp=mpart, b=bias, d=dst:
                      e.activation(d, ps[0:mp, :], AF.Identity, bias=b), inc="s")
            a1r.drained(si, "s", vv)
        return tv

    B.wait("tensor", "g", g_w1)
    B.wait("tensor", "hw", hw0)
    for kb in range(NKB):
        B.wait("tensor", "g", hwh[kb])
        rhs = hb[kb % 2]
        for ci in range(CC):
            t_hb[kb % 2] = a1_group(rhs, wA_dkv_sb, 512, ci * 128, 128, b_dkv[:, ci:ci + 1],
                                    ckv_sb[:, ci * KEYS + kb * 256: ci * KEYS + kb * 256 + 256])
        t_hb[kb % 2] = a1_group(rhs, wA_kr_sb, 64, 0, 64, b_kr[:],
                                krw_sb[0:64, kb * 256:kb * 256 + 256])
        if kb + 2 < NKB:
            B.wait("gpsimd", "t", t_hb[kb % 2])
            hwh.append(g_dma(hb[kb % 2], inp["h_kb"][kb + 2]))
    hwq = []
    for i in range(2):
        B.wait("gpsimd", "t", t_hb[i])
        hwq.append(g_dma(hb[i], inp["h_q_kb"][i]))
    t_hbq = [0, 0]
    for kb in range(8):
        B.wait("tensor", "g", hwq[kb])
        rhs = hb[kb % 2]
        for ci in range(CC):
            t_hbq[kb % 2] = a1_group(rhs, wA_dq_sb, 512, ci * 128, 128, b_dq[:, ci:ci + 1],
                                     cq_sb[:, ci * TOK + kb * 256: ci * TOK + kb * 256 + 256])
        if kb + 2 < 8:
            B.wait("gpsimd", "t", t_hbq[kb % 2])
            hwq.append(g_dma(hb[kb % 2], inp["h_q_kb"][kb + 2]))
    t_a1_done = B.cnt["t"]
    v_a1_done = B.cnt["v"]

    if dbg_ckv is not None:
        B.wait("gpsimd", "v", v_a1_done)
        g_dma(dbg_ckv, ckv_sb)
        g_dma(dbg_cq, cq_sb)

    if phases <= 1:
        B.wait("sync", "v", B.cnt["v"])
        B.wait("sync", "hw", B.cnt["hw"])
        B.wait("gpsimd", "g", B.cnt["g"])
        return B.finalize()

    B.wait("gpsimd", "t", t_a1_done)
    g_up = 0
    for dst, src in [(wA_upk_sb, inp["wA_upk"]), (wA_upv_sb, inp["wA_upv"]),
                     (wA_qcqr_sb, inp["wA_qcqr"])]:
        g_up = g_dma(dst, src)
    B.wait("sync", "t", t_a1_done)
    B.wait("sync", "v", v_a1_done)
    hw_tab = 0
    for dst, src in [(Tq_sin, inp["Tq_sin"]), (Tq_cos, inp["Tq_cos"]),
                     (b_upv_sb, inp["b_upv"])]:
        hw_tab = s_dma(dst, src)

    t_rot_drain = 0
    v_tk_read = 0
    for khalf in range(2):
        B.wait("sync", "v", v_tk_read)
        hw_tk = s_dma(Tk_sin, inp["Tk_sin"][:, khalf * 2048:(khalf + 1) * 2048])
        hw_tk = s_dma(Tk_cos, inp["Tk_cos"][:, khalf * 2048:(khalf + 1) * 2048])
        for nth in range(4):
            nt = khalf * 4 + nth
            B.wait("tensor", "v", t_rot_drain)
            tv = B.op("tensor", lambda e, nt=nt: e.matmul(
                PE[0:64, 0:512], R64T[:], krw_sb[0:64, nt * 512:(nt + 1) * 512],
                start=True, stop=True), inc="t")
            B.wait("vector", "t", tv)
            if nth == 0:
                B.wait("vector", "hw", hw_tk)
            B.op("vector", lambda e, nt=nt, nth=nth: e.tensor_tensor(
                krsc[0], krw_sb[0:64, nt * 512:(nt + 1) * 512],
                Tk_sin[:, nth * 512:(nth + 1) * 512], ALU.mult))
            vop(lambda e, nth=nth: e.tensor_tensor(
                krsc[1], PE[0:64, 0:512],
                Tk_cos[:, nth * 512:(nth + 1) * 512], ALU.mult))
            B.op("vector", lambda e, nt=nt: e.tensor_tensor(
                Kb[0][64:128, nt * 512:(nt + 1) * 512], krsc[0],
                krsc[1], ALU.add))
            vv = B.op("vector", lambda e, nt=nt: e.tensor_copy(
                Kb[1][64:128, nt * 512:(nt + 1) * 512],
                Kb[0][64:128, nt * 512:(nt + 1) * 512]), inc="v")
            t_rot_drain = vv
        v_tk_read = t_rot_drain
    v_krr = B.cnt["v"]
    if dbg_krr is not None:
        B.wait("gpsimd", "v", v_krr)
        g_dma(dbg_krr, Kb[0])

    tK = [0, 0]
    tQ = [0, 0]
    tV = 0
    vOT = 0
    sLB = 0
    tLB = 0
    tP = [0, 0]
    ait = [0]
    obanks = [PE, PF]
    o_free = [("v", 0), ("v", 0)]
    v_lastevac = [0]
    lrow_tmm = [0, 0]
    sPA = [0]
    pending_lb = [None]

    s_ring = Ring(B, [PB, PC])
    qb_slots = [lambda: PA[:, 0:512], lambda: PA[:, 512:1024],
                lambda: PB[:, 0:512], lambda: PB[:, 512:1024]]
    qbr = Ring(B, qb_slots)

    B.wait("tensor", "g", g_up)
    B.wait("tensor", "v", v_krr)
    B.wait("tensor", "v", v_vones)
    B.wait("tensor", "hw", hw_tab)

    def build_group(w_sb, w_stride, w_off, mpart, rhs_sb, rhs_stride, rhs_off, rhs_n):
        si = qbr.next()
        ps = qbr.ap(si)
        tv = None
        for c in range(CC):
            def mm(e, ps=ps, c=c, w=w_sb, ws=w_stride, wo=w_off, mp=mpart,
                   r=rhs_sb, rs=rhs_stride, ro=rhs_off, n=rhs_n,
                   first=(c == 0), last=(c == CC - 1)):
                return e.matmul(ps[0:mp, 0:n], w[:, c * ws + wo:c * ws + wo + mp],
                                r[:, c * rs + ro:c * rs + ro + n], start=first, stop=last)
            tv = B.op("tensor", mm, inc="t" if c == CC - 1 else None)
        return si, ps, tv

    for p in range(NPAIR):
        h0, h1 = 2 * p, 2 * p + 1
        if pending_lb[0] is not None:
            pending_lb[0]()
            pending_lb[0] = None
        B.wait("tensor", "s", sPA[0])
        for _slot in range(2):
            _sem, _val = s_ring.free_at[_slot]
            B.wait("tensor", _sem, _val)
        for nt in range(8):
            si, ps, tv = build_group(wA_upk_sb, 1024, p * 128, 128, ckv_sb, KEYS, nt * 512, 512)
            B.wait("scalar", "t", tv)
            B.wait("scalar", "t", tK[0])
            B.op("scalar", lambda e, ps=ps, nt=nt, p=p: e.activation(
                Kb[0][0:64, nt * 512:(nt + 1) * 512], ps[0:64, :],
                AF.Identity, bias=b_upk[0:64, 2 * p:2 * p + 1]))
            B.wait("scalar", "t", tK[1])
            vv = B.op("scalar", lambda e, ps=ps, nt=nt, p=p: e.activation(
                Kb[1][0:64, nt * 512:(nt + 1) * 512], ps[64:128, :],
                AF.Identity, bias=b_upk[64:128, 2 * p + 1:2 * p + 2]), inc="s")
            qbr.drained(si, "s", vv)
        vK = ("s", B.cnt["s"])

        sem, val = o_free[1]
        B.wait("tensor", sem, val)
        tv = B.op("tensor", lambda e, p=p: e.matmul(
            PF[:, 0:128], ones1f[:], b_upv_sb[:, p * 128:(p + 1) * 128],
            start=True, stop=True), inc="t")
        B.wait("scalar", "t", tv)
        sbvt = B.op("scalar", lambda e: e.copy(bvt[:], PF[:, 0:128]), inc="s")
        o_free[1] = ("s", sbvt)
        for tc in range(32):
            si, ps, tv = build_group(ckv_sb, KEYS, tc * 128, 128, wA_upv_sb, 1024, p * 128, 128)
            B.wait("vector", "t", tv)
            if tc == 0:
                B.wait("vector", "t", tV)
                B.wait("vector", "s", sbvt)
            B.op("vector", lambda e, ps=ps, tc=tc: e.tensor_tensor(
                Vp[:, tc * 130:tc * 130 + 64], ps[:, 0:64], bvt[:, 0:64], ALU.add))
            vv = B.op("vector", lambda e, ps=ps, tc=tc: e.tensor_tensor(
                Vp[:, tc * 130 + 65:tc * 130 + 129], ps[:, 64:128], bvt[:, 64:128],
                ALU.add), inc="v")
            qbr.drained(si, "v", vv)
        vV = B.cnt["v"]

        vQ = [0, 0]
        for hi, h in enumerate((h0, h1)):
            for nt in range(4):
                si, ps, tv = build_group(wA_qcqr_sb, 2048, h * 128, 128, cq_sb, TOK, nt * 512, 512)
                B.wait("scalar", "t", tv)
                B.wait("scalar", "t", tQ[hi])
                vv = B.op("scalar", lambda e, ps=ps, nt=nt, h=h, hi=hi: e.activation(
                    Qh[hi][:, nt * 512:(nt + 1) * 512], ps[:],
                    AF.Identity, bias=b_qcqr[:, h:h + 1]), inc="s")
                qbr.drained(si, "s", vv)
            v_qraw = ("s", B.cnt["s"])
            for nt in range(4):
                B.wait("tensor", v_qraw[0], v_qraw[1])
                sem, val = o_free[0]
                B.wait("tensor", sem, val)
                tv = B.op("tensor", lambda e, hi=hi, nt=nt: e.matmul(
                    PE[:, 0:512], RqT[:], Qh[hi][:, nt * 512:(nt + 1) * 512],
                    start=True, stop=True), inc="t")
                B.op("vector", lambda e, hi=hi, nt=nt: e.tensor_tensor(
                    rsc[64:128, 0:512], Qh[hi][64:128, nt * 512:(nt + 1) * 512],
                    Tq_sin[64:128, nt * 512:(nt + 1) * 512], ALU.mult))
                B.wait("vector", "t", tv)
                vop(lambda e, hi=hi, nt=nt: e.tensor_tensor(
                    Qh[hi][64:128, nt * 512:(nt + 1) * 512], PE[64:128, 0:512],
                    Tq_cos[64:128, nt * 512:(nt + 1) * 512], ALU.mult))
                vv = B.op("vector", lambda e, hi=hi, nt=nt: e.tensor_tensor(
                    Qh[hi][64:128, nt * 512:(nt + 1) * 512],
                    Qh[hi][64:128, nt * 512:(nt + 1) * 512],
                    rsc[64:128, 0:512], ALU.add), inc="v")
                o_free[0] = ("v", vv)
            vQ[hi] = B.cnt["v"]

        if p == 0 and dbg_K is not None:
            B.wait("gpsimd", "v", B.cnt["v"])
            g_dma(dbg_K[0], Kb[0])
            g_dma(dbg_K[1], Kb[1])
            g_dma(dbg_V, Vp)
            g_dma(dbg_Q[0], Qh[0])
            g_dma(dbg_Q[1], Qh[1])

        for hi, h in enumerate((h0, h1)):
            for qt in range(4):
                obank = obanks[ait[0] % 2]
                t_pv_last = None

                def emit_s(cg, hi=hi, qt=qt):
                    si = s_ring.next()
                    ps = s_ring.ap(si)
                    tv = None
                    for j in range(2):
                        kcix = 2 * cg + j
                        def smm(e, ps=ps, j=j, kcix=kcix, hi=hi, qt=qt):
                            return e.matmul(ps[:, j * 512:(j + 1) * 512],
                                            Kb[hi][:, kcix * 128:(kcix + 1) * 128],
                                            Qh[hi][:, qt * 512:(qt + 1) * 512],
                                            start=True, stop=True)
                        tv = B.op("tensor", smm, inc="t" if j == 1 else None)
                    return si, ps, tv

                B.wait("tensor", vK[0], vK[1])
                B.wait("tensor", "v", vQ[hi])
                B.wait("tensor", "v", vV)
                s_pend = emit_s(0)
                for cg in range(16):
                    si, ps, tv = s_pend
                    tK[hi] = tv
                    tQ[hi] = tv
                    slot = cg % 2
                    B.wait("scalar", "t", tv)
                    B.wait("scalar", "t", tP[slot])
                    sv = B.op("scalar", lambda e, ps=ps, slot=slot: e.activation(
                        Pr[:, slot * 1024:(slot + 1) * 1024], ps[:], AF.Exp,
                        bias=0.0, scale=SCL), inc="s")
                    s_ring.drained(si, "s", sv)
                    if cg + 1 < 16:
                        s_pend = emit_s(cg + 1)
                    if cg == 0 and pending_lb[0] is not None:
                        pending_lb[0]()
                        pending_lb[0] = None
                    B.wait("tensor", "s", sv)
                    if cg == 0:
                        sem, val = o_free[ait[0] % 2]
                        B.wait("tensor", sem, val)
                    for j in range(2):
                        kcix = 2 * cg + j
                        def pvmm(e, j=j, kcix=kcix, hi=hi, slot=slot, ob=obank,
                                 first=(cg == 0 and j == 0), last=(cg == 15 and j == 1)):
                            return e.matmul(
                                ob[:, 0:512],
                                Vp[:, kcix * 130 + hi * 65:kcix * 130 + hi * 65 + 128],
                                Pr[:, slot * 1024 + j * 512:slot * 1024 + (j + 1) * 512],
                                start=first, stop=last)
                        t_pv_last = B.op("tensor", pvmm, inc="t" if j == 1 else None)
                    tP[slot] = t_pv_last
                    tV = t_pv_last
                lslot = ait[0] % 2
                B.wait("vector", "t", t_pv_last)
                B.wait("vector", "t", lrow_tmm[lslot])
                vl = B.op("vector", lambda e, ob=obank, lslot=lslot: e.tensor_copy(
                    lrow[lslot], ob[64:65, 0:512]), inc="v")
                B.wait("scalar", "v", vl)
                s_ln = B.op("scalar", lambda e, lslot=lslot: e.activation(
                    lrow_ln[lslot], lrow[lslot], AF.Ln), inc="s")

                def make_lb(s_ln=s_ln, lslot=lslot, ob=obank, h=h, qt=qt, oslot=ait[0] % 2):
                    def emit():
                        B.wait("tensor", "s", s_ln)
                        B.wait("tensor", "s", sPA[0])
                        tlb = B.op("tensor", lambda e: e.matmul(
                            PA[0:64, 0:512], ones1f[:, 0:64], lrow_ln[lslot],
                            start=True, stop=True), inc="t")
                        lrow_tmm[lslot] = tlb
                        B.wait("scalar", "t", tlb)
                        B.wait("scalar", "v", v_lastevac[0])
                        slb = B.op("scalar", lambda e: e.activation(
                            lb_sb, PA[0:64, 0:512], AF.Exp, scale=-1.0), inc="s")
                        sPA[0] = slb
                        B.wait("vector", "s", slb)
                        oc = h // 2
                        r0 = (h % 2) * 64
                        vev = B.op("vector", lambda e, oc=oc, r0=r0, qt=qt, ob=ob:
                                   e.tensor_tensor(
                                       OT_sb[r0:r0 + 64,
                                             oc * 2048 + qt * 512:oc * 2048 + (qt + 1) * 512],
                                       ob[0:64, 0:512], lb_sb, ALU.mult), inc="v")
                        o_free[oslot] = ("v", vev)
                        v_lastevac[0] = vev
                    return emit
                pending_lb[0] = make_lb()
                ait[0] += 1
    if pending_lb[0] is not None:
        pending_lb[0]()
        pending_lb[0] = None
    v_a2_done = B.cnt["v"]
    t_a2_done = B.cnt["t"]
    s_a2_done = B.cnt["s"]
    if dbg_OT is not None:
        B.wait("gpsimd", "v", v_a2_done)
        g_dma(dbg_OT, OT_sb)

    if phases <= 2:
        B.wait("sync", "v", B.cnt["v"])
        B.wait("sync", "hw", B.cnt["hw"])
        B.wait("gpsimd", "g", B.cnt["g"])
        return B.finalize()

    B.wait("gpsimd", "t", t_a2_done)
    g_fc = g_dma(wA_fc_sb, inp["wA_fc"])
    B.wait("sync", "t", t_a2_done)
    B.wait("sync", "v", v_a2_done)
    hw_bfc = s_dma(b_fc_sb, inp["b_fc"])
    B.wait("tensor", "hw", hw_bfc)
    B.wait("tensor", "v", v_a2_done)
    B.wait("tensor", "s", sPA[0])
    sLB = 0
    for nt in range(4):
        t_b = B.op("tensor", lambda e, nt=nt: e.matmul(
            PF[:, 0:512], ones1f[:], b_fc_sb[:, nt * 512:(nt + 1) * 512],
            start=True, stop=True), inc="t")
        B.wait("scalar", "t", t_b)
        sLB = B.op("scalar", lambda e, nt=nt: e.copy(bfcB[:, nt * 512:(nt + 1) * 512],
                                                     PF[:, 0:512]), inc="s")
        B.wait("tensor", "s", sLB)
    s_bfc = B.cnt["s"]

    hwho = [s_dma(hob[i], inp["h_own"][i]) for i in range(2)]
    pending_tr = [None]
    str_free = [0, 0, 0]
    PFb = PF.bitcast(BF16)
    tr_out = [PEb, PFb]
    fc_ring = Ring(B, [PB, PC])
    fc_ring.free_at = [("s", s_a2_done), ("s", s_a2_done)]
    v_tr_drain = [0, 0, 0, 0]
    hw_x1 = [0, 0]
    v_hob = [0, 0]

    B.wait("tensor", "g", g_fc)
    B.wait("tensor", "s", s_bfc)
    B.wait("tensor", "v", v_a2_done)
    B.wait("scalar", "t", t_a2_done)
    for qt in range(NQT if A3_SUB > 0 else 0):
        tg = []
        for g in range(2):
            si = fc_ring.next()
            ps = fc_ring.ap(si)
            tv = None
            for sub in range(2):
                nt = 2 * g + sub
                if g == 1 and sub == 0 and pending_tr[0] is not None:
                    pending_tr[0]()
                    pending_tr[0] = None
                for oc in range(8):
                    def fmm(e, ps=ps, sub=sub, nt=nt, oc=oc, qt=qt,
                            first=(oc == 0), last=(oc == 7)):
                        return e.matmul(
                            ps[:, sub * 512:(sub + 1) * 512],
                            OT_sb[:, oc * 2048 + qt * 128:oc * 2048 + qt * 128 + 128],
                            wA_fc_sb[:, oc * 2048 + nt * 512:oc * 2048 + (nt + 1) * 512],
                            start=first, stop=last)
                    tv = B.op("tensor", fmm, inc="t" if oc == 7 else None)
            tg.append((si, ps, tv))
        if A3_SUB < 2:
            for g, (si, ps, tv) in enumerate(tg):
                B.wait("vector", "t", tv)
                vv = B.op("vector", lambda e, ps=ps: e.tensor_copy(
                    z_sb[:, 0:1024], ps[:]), inc="v")
                fc_ring.drained(si, "v", vv)
            continue
        B.wait("vector", "hw", hwho[qt])
        vv = None
        for g, (si, ps, tv) in enumerate(tg):
            B.wait("vector", "t", tv)
            vop(lambda e, ps=ps, g=g: e.scalar_tensor_tensor(
                z_sb[:, g * 1024:(g + 1) * 1024], ps[:], 1.0,
                bfcB[:, g * 1024:(g + 1) * 1024], ALU.mult, ALU.add))
            vv = B.op("vector", lambda e, g=g, qt=qt: e.tensor_tensor(
                z_sb[:, g * 1024:(g + 1) * 1024], z_sb[:, g * 1024:(g + 1) * 1024],
                hob[qt % 2][:, g * 1024:(g + 1) * 1024], ALU.add), inc="v")
            fc_ring.drained(si, "v", vv)
        v_hob[qt % 2] = vv
        v_z = vv
        if qt + 2 < NQT:
            B.wait("sync", "v", v_hob[qt % 2])
            hwho.append(s_dma(hob[qt % 2], inp["h_own"][qt + 2]))
        if A3_SUB < 3:
            continue
        B.wait("scalar", "v", v_z)
        B.wait("scalar", "hw", hw_x1[qt % 2])
        sq = B.op("scalar", lambda e, qt=qt: e.activation(
            x1f[qt % 2][:], z_sb[:], AF.Square, accum_out=smll[:, 0:1]), inc="s")
        vop(lambda e: e.reduce_sum(smll[:, 1:2], z_sb[:], axis=AX.X))
        B.wait("vector", "s", sq)
        vop(lambda e: e.tensor_scalar(smll[:, 2:3], smll[:, 1:2], 1.0 / D, None, ALU.mult))
        vop(lambda e: e.tensor_scalar(smll[:, 3:4], smll[:, 0:1], 1.0 / D, None, ALU.mult))
        vop(lambda e: e.tensor_tensor(smll[:, 4:5], smll[:, 2:3], smll[:, 2:3], ALU.mult))
        vv = B.op("vector", lambda e: e.tensor_tensor(smll[:, 5:6], smll[:, 3:4], smll[:, 4:5], ALU.subtract), inc="v")
        B.wait("vector", "v", vv)
        B.wait("scalar", "v", vv)
        sv = B.op("scalar", lambda e: e.activation(smll[:, 6:7], smll[:, 5:6], AF.Sqrt, bias=eps_col[:]), inc="s")
        B.wait("vector", "s", sv)
        vop(lambda e: e.reciprocal(smll[:, 7:8], smll[:, 6:7]))
        vop(lambda e: e.tensor_scalar(smll[:, 8:9], smll[:, 2:3], smll[:, 7:8], -1.0, ALU.mult, ALU.mult))
        vx1 = B.op("vector", lambda e, qt=qt: e.tensor_scalar(
            x1f[qt % 2][:], z_sb[:], smll[:, 7:8], smll[:, 8:9], ALU.mult, ALU.add), inc="v")
        if qt == 0 and dbg_z is not None:
            B.wait("sync", "v", vx1)
            s_dma(dbg_z, z_sb[:])
            s_dma(dbg_bfcB, hob[qt % 2][:])
            s_dma(dbg_stats, smll[:])
            dmpv = s_dma(nc.dram_tensor("dbg_bfc2", [128, 2048], F32, kind="ExternalOutput").ap(), bfcB[:])
            B.wait("vector", "hw", dmpv)
            B.wait("scalar", "hw", dmpv)
            B.wait("sync", "hw", dmpv)
        if A3_SUB < 4:
            continue
        B.wait("sync", "v", vx1)
        hw_x1[qt % 2] = s_dma(x1_hbm[qt * 128:(qt + 1) * 128, :], x1f[qt % 2])
        if A3_SUB < 5:
            continue
        B.wait("scalar", "v", vx1)
        stage = x1bf_sb if qt == 0 else (x1bf_b if qt % 2 else x1bf_c)
        B.wait("scalar", "t", str_free[0 if qt == 0 else (1 if qt % 2 else 2)])
        scast = B.op("scalar", lambda e, qt=qt, st=stage: e.copy(st, x1f[qt % 2][:]), inc="s")
        if qt == 0:
            scast_q0 = scast
            continue

        def make_tr(scast=scast, qt=qt, stage=stage, sslot=(1 if qt % 2 else 2)):
            def emit():
                for c in range(DC if A3_SUB >= 6 else 0):
                    trslot = c % 2
                    if c == 0:
                        B.wait("tensor", "s", scast)
                        B.wait("tensor", "s", sLB if qt == 1 else 0)
                    B.wait("tensor", "v", v_tr_drain[trslot])
                    tv = B.op("tensor", lambda e, c=c, trslot=trslot, stage=stage:
                              e.transpose(tr_out[trslot][:, 0:128],
                                          stage[:, c * 128:(c + 1) * 128], I128[:]),
                              inc="t")
                    str_free[sslot] = tv
                    B.wait("vector", "t", tv)
                    vv = B.op("vector", lambda e, c=c, qt=qt, trslot=trslot:
                              e.tensor_copy(
                                  xT_sb[:, c * 2048 + qt * 128:c * 2048 + qt * 128 + 128],
                                  tr_out[trslot][:, 0:128]), inc="v")
                    v_tr_drain[trslot] = vv
            return emit
        pending_tr[0] = make_tr()
    if pending_tr[0] is not None:
        pending_tr[0]()
        pending_tr[0] = None
    if A3_SUB >= 6:
        for c in range(DC):
            trslot = c % 2
            if c == 0:
                B.wait("tensor", "s", scast_q0)
            B.wait("tensor", "v", v_tr_drain[trslot])
            tv = B.op("tensor", lambda e, c=c, trslot=trslot: e.transpose(
                tr_out[trslot][:, 0:128], x1bf_sb[:, c * 128:(c + 1) * 128],
                I128[:]), inc="t")
            B.wait("vector", "t", tv)
            vv = B.op("vector", lambda e, c=c, trslot=trslot: e.tensor_copy(
                xT_sb[:, c * 2048 + 0 * 128:c * 2048 + 0 * 128 + 128],
                tr_out[trslot][:, 0:128]), inc="v")
            v_tr_drain[trslot] = vv
    v_a3_done = B.cnt["v"]
    t_a3_done = B.cnt["t"]
    hw_a3_done = B.cnt["hw"]

    if phases <= 3:
        B.wait("sync", "v", B.cnt["v"])
        B.wait("sync", "hw", B.cnt["hw"])
        B.wait("gpsimd", "g", B.cnt["g"])
        return B.finalize()

    B.wait("gpsimd", "t", t_a3_done)
    g_wr = g_dma(wR_sb, inp["wR"])
    B.wait("tensor", "g", g_wr)
    B.wait("tensor", "v", v_a3_done)
    s_sig = 0
    for nt in range(4):
        tv = None
        B.wait("tensor", "s", s_sig)
        for c in range(DC):
            def rmm(e, c=c, nt=nt, first=(c == 0), last=(c == DC - 1)):
                return e.matmul(PF[0:8, 0:512], wR_sb[:, c * 8:(c + 1) * 8],
                                xT_sb[:, c * 2048 + nt * 512:c * 2048 + (nt + 1) * 512],
                                start=first, stop=last)
            tv = B.op("tensor", rmm, inc="t" if c == DC - 1 else None)
        B.wait("scalar", "t", tv)
        B.wait("scalar", "hw", hw_a3_done)
        s_sig = B.op("scalar", lambda e, nt=nt: e.activation(
            sigT[:, nt * 512:(nt + 1) * 512], PF[0:8, 0:512], AF.Sigmoid), inc="s")
    v_cm = 0
    t_cmt = 0
    for qt in range(NQT):
        B.wait("tensor", "s", s_sig)
        B.wait("tensor", "v", v_cm)
        tv = B.op("tensor", lambda e, qt=qt: e.transpose(
            PE[:, 0:8], sigT[:, qt * 128:(qt + 1) * 128], I128f[0:8, 0:8]), inc="t")
        B.wait("vector", "t", tv)
        B.wait("vector", "hw", hw_a3_done)
        vop(lambda e: e.tensor_copy(stok[:, 0:8], PE[:, 0:8]))
        vop(lambda e: e.max(out=top8[:], in_=stok[:, 0:8]))
        vop(lambda e: e.tensor_tensor(smll[:, 0:1], top8[:, 0:1], top8[:, 1:2], ALU.add))
        vop(lambda e: e.reciprocal(smll[:, 1:2], smll[:, 0:1]))
        vop(lambda e: e.tensor_scalar(smll[:, 2:3], top8[:, 0:1], smll[:, 1:2], None, ALU.mult))
        vop(lambda e: e.tensor_scalar(smll[:, 3:4], top8[:, 1:2], smll[:, 1:2], None, ALU.mult))
        vop(lambda e: e.tensor_scalar(cmb_sc[:, 0:8], stok[:, 0:8], top8[:, 0:1], smll[:, 2:3], ALU.is_equal, ALU.mult))
        vop(lambda e: e.tensor_scalar(cmb_sc[:, 8:16], stok[:, 0:8], top8[:, 1:2], smll[:, 3:4], ALU.is_equal, ALU.mult))
        v_cm = B.op("vector", lambda e, qt=qt: e.tensor_tensor(
            cmb_all[:, qt * 8:(qt + 1) * 8], cmb_sc[:, 0:8], cmb_sc[:, 8:16], ALU.add),
            inc="v")
    v_f0_done = B.cnt["v"]
    if dbg_cmb is not None:
        B.wait("sync", "v", v_f0_done)
        s_dma(dbg_cmb, cmb_all[:])

    if phases <= 4:
        B.wait("sync", "v", B.cnt["v"])
        B.wait("sync", "hw", B.cnt["hw"])
        B.wait("gpsimd", "g", B.cnt["g"])
        return B.finalize()

    g_w1r = [0, 0]
    g_w3r = [0, 0]
    g_w2 = 0
    t_h1s_read = 0
    s_ystage = [0, 0]
    g_ystage = [0, 0]

    B.wait("gpsimd", "t", t_a3_done)
    g_w2 = g_dma(W2s_sb, inp["W2L"][0])
    for r in range(2):
        g_w1r[r] = g_dma(w1ring[r], inp["W1L"][0][r])
        g_w3r[r] = g_dma(w3ring[r], inp["W3L"][0][r])

    f1_slots = [(PA[:, 0:512], PA[:, 512:1024]),
                (PB[:, 0:512], PB[:, 512:1024]),
                (PE[:, 0:512], PF[:, 0:512])]
    f1_ring = Ring(B, f1_slots)
    f1_ring.free_at = [("v", v_a2_done), ("v", v_a3_done), ("v", v_f0_done)]
    y_ring = Ring(B, [PC, PA])
    y_ring.free_at = [("v", v_a3_done), ("v", 0)]

    for p in range(NPASS_RUN):
        for f in range(NF):
            r = (p * NF + f) % 2
            B.wait("tensor", "g", g_w1r[r])
            B.wait("tensor", "g", g_w3r[r])
            for tt in range(4):
                si = f1_ring.next()
                psA, psB = f1_ring.ap(si)
                tv = None
                for c in range(DC):
                    def m1(e, ps=psA, c=c, r=r, tt=tt, first=(c == 0), last=(c == DC - 1)):
                        return e.matmul(ps, w1ring[r][:, c * 128:(c + 1) * 128],
                                        xT_sb[:, c * 2048 + tt * 512:c * 2048 + (tt + 1) * 512],
                                        start=first, stop=last)
                    B.op("tensor", m1)
                for c in range(DC):
                    def m3(e, ps=psB, c=c, r=r, tt=tt, first=(c == 0), last=(c == DC - 1)):
                        return e.matmul(ps, w3ring[r][:, c * 128:(c + 1) * 128],
                                        xT_sb[:, c * 2048 + tt * 512:c * 2048 + (tt + 1) * 512],
                                        start=first, stop=last)
                    tv = B.op("tensor", m3, inc="t" if c == DC - 1 else None)
                B.wait("scalar", "t", tv)
                sv = B.op("scalar", lambda e, ps=psA: e.activation(
                    tsil[:], ps, AF.Silu), inc="s")
                B.wait("vector", "s", sv)
                if p > 0:
                    B.wait("vector", "t", t_h1s_read)
                vv = B.op("vector", lambda e, ps=psB, f=f, tt=tt: e.tensor_tensor(
                    h1s_sb[:, f * 2048 + tt * 512:f * 2048 + (tt + 1) * 512],
                    tsil[:], ps, ALU.mult), inc="v")
                f1_ring.drained(si, "v", vv)
            nf_, np_ = f + 2, p
            if nf_ >= NF:
                nf_ -= NF
                np_ += 1
            if np_ < NPASS:
                t_used = B.cnt["t"]
                B.wait("gpsimd", "t", t_used)
                g_w1r[r] = g_dma(w1ring[r], inp["W1L"][np_][nf_])
                g_w3r[r] = g_dma(w3ring[r], inp["W3L"][np_][nf_])
        v_f1 = B.cnt["v"]
        if p == 0 and dbg_h1s is not None:
            B.wait("gpsimd", "v", v_f1)
            g_dma(dbg_h1s, h1s_sb)

        B.wait("tensor", "g", g_w2)
        B.wait("tensor", "v", v_f1)
        y_ring.free_at[1] = f1_ring.free_at[0]
        for tt in range(NQT):
            for g2 in range(2):
                si = y_ring.next()
                ps = y_ring.ap(si)
                tv = None
                for sub in range(2):
                    nt = 2 * g2 + sub
                    for f in range(NF):
                        def ymm(e, ps=ps, sub=sub, nt=nt, f=f, tt=tt,
                                first=(f == 0), last=(f == NF - 1)):
                            return e.matmul(
                                ps[:, sub * 512:(sub + 1) * 512],
                                h1s_sb[:, f * 2048 + tt * 128:f * 2048 + tt * 128 + 128],
                                W2s_sb[:, f * 2048 + nt * 512:f * 2048 + (nt + 1) * 512],
                                start=first, stop=last)
                        tv = B.op("tensor", ymm, inc="t" if f == NF - 1 else None)
                yslot = (tt * 2 + g2) % 2
                if p < 8:
                    B.wait("vector", "t", tv)
                    B.wait("vector", "g", g_ystage[yslot])
                    if p == 0:
                        B.wait("vector", "v", v_f0_done)
                    sv = B.op("vector", lambda e, ps=ps, yslot=yslot, tt=tt, p=p:
                              e.tensor_scalar(ystage[yslot][:], ps[:],
                                              cmb_all[:, tt * 8 + p:tt * 8 + p + 1],
                                              None, ALU.mult), inc="v")
                    y_ring.drained(si, "v", sv)
                    B.wait("gpsimd", "v", sv)
                else:
                    B.wait("scalar", "t", tv)
                    B.wait("scalar", "g", g_ystage[yslot])
                    sv = B.op("scalar", lambda e, ps=ps, yslot=yslot: e.copy(
                        ystage[yslot][:], ps[:]), inc="s")
                    y_ring.drained(si, "s", sv)
                    B.wait("gpsimd", "s", sv)
                if si == 1:
                    f1_ring.free_at[0] = y_ring.free_at[si]
                s_ystage[yslot] = sv
                g_ystage[yslot] = g_dma(
                    y_hbm[tt * 128:(tt + 1) * 128, g2 * 1024:(g2 + 1) * 1024],
                    ystage[yslot], accum=(p > 0))
        t_h1s_read = B.cnt["t"]
        if p + 1 < NPASS:
            B.wait("gpsimd", "t", t_h1s_read)
            g_w2 = g_dma(W2s_sb, inp["W2L"][p + 1])
    g_y_done = B.cnt["g"]
    s_f2_done = B.cnt["s"]

    if phases <= 5:
        B.wait("sync", "v", B.cnt["v"])
        B.wait("sync", "hw", B.cnt["hw"])
        B.wait("gpsimd", "g", B.cnt["g"])
        return B.finalize()

    B.wait("sync", "g", g_y_done)
    B.wait("sync", "s", s_f2_done)
    hwx = []
    hwy = []
    for i in range(2):
        hwx.append(s_dma(x1b[i], x1_hbm[i * 128:(i + 1) * 128, :]))
        hwy.append(s_dma(yb[i], y_hbm[i * 128:(i + 1) * 128, :]))
    hw_ob = [0, 0]
    for qt in range(NQT):
        sl = qt % 2
        B.wait("vector", "hw", hwx[qt])
        B.wait("vector", "hw", hwy[qt])
        B.wait("vector", "t", t_h1s_read)
        vz = B.op("vector", lambda e, sl=sl: e.tensor_tensor(
            x1b[sl][:], x1b[sl][:], yb[sl][:], ALU.add), inc="v")
        B.wait("vector", "v", vz)
        B.wait("scalar", "v", vz)
        B.wait("scalar", "hw", hw_ob[sl])
        sq = B.op("scalar", lambda e, sl=sl: e.activation(
            outb[sl][:], x1b[sl][:], AF.Square, accum_out=smll[:, 0:1]), inc="s")
        vop(lambda e, sl=sl: e.reduce_sum(smll[:, 1:2], x1b[sl][:], axis=AX.X))
        B.wait("vector", "s", sq)
        vop(lambda e: e.tensor_scalar(smll[:, 2:3], smll[:, 1:2], 1.0 / D, None, ALU.mult))
        vop(lambda e: e.tensor_scalar(smll[:, 3:4], smll[:, 0:1], 1.0 / D, None, ALU.mult))
        vop(lambda e: e.tensor_tensor(smll[:, 4:5], smll[:, 2:3], smll[:, 2:3], ALU.mult))
        vv = B.op("vector", lambda e: e.tensor_tensor(smll[:, 5:6], smll[:, 3:4], smll[:, 4:5], ALU.subtract), inc="v")
        B.wait("vector", "v", vv)
        B.wait("scalar", "v", vv)
        sv = B.op("scalar", lambda e: e.activation(smll[:, 6:7], smll[:, 5:6], AF.Sqrt, bias=eps_col[:]), inc="s")
        B.wait("vector", "s", sv)
        vop(lambda e: e.reciprocal(smll[:, 7:8], smll[:, 6:7]))
        vop(lambda e: e.tensor_scalar(smll[:, 8:9], smll[:, 2:3], smll[:, 7:8], -1.0, ALU.mult, ALU.mult))
        vo = B.op("vector", lambda e, sl=sl: e.tensor_scalar(
            outb[sl][:], x1b[sl][:], smll[:, 7:8], smll[:, 8:9], ALU.mult, ALU.add), inc="v")
        B.wait("sync", "v", vo)
        hw_ob[sl] = s_dma(out_hbm[qt * 128:(qt + 1) * 128, :], outb[sl])
        if qt + 2 < NQT:
            B.wait("sync", "v", vo)
            hwx.append(s_dma(x1b[sl], x1_hbm[(qt + 2) * 128:(qt + 3) * 128, :]))
            hwy.append(s_dma(yb[sl], y_hbm[(qt + 2) * 128:(qt + 3) * 128, :]))
    B.wait("sync", "hw", B.cnt["hw"])
    B.wait("gpsimd", "g", B.cnt["g"])
    B.wait("vector", "hw", B.cnt["hw"])

    return B.finalize()


_prog_cache = {}


def get_program(debug=()):
    key = tuple(sorted(debug))
    if key not in _prog_cache:
        _prog_cache[key] = build_program(debug=key)
    return _prog_cache[key]


def kernel(**inputs):
    shared = prep_shared(inputs)
    in_maps = [prep_core(inputs, shared, c) for c in range(8)]
    nc = get_program()
    res = run_bass_kernel_spmd(nc, in_maps, core_ids=list(range(8)))
    out = np.empty((4, 4096, 2048), dtype=np.float32)
    for c in range(8):
        b, half = c // 2, c % 2
        out[b, half * TOK:(half + 1) * TOK, :] = res.results[c]["out"]
    return out


# version 08
# speedup vs baseline: 1.0150x; 1.0150x over previous
import math

import numpy as np
import ml_dtypes

import concourse.bass as bass
import concourse.mybir as mybir
from concourse.bass_utils import run_bass_kernel_spmd

F32 = mybir.dt.float32
BF16 = mybir.dt.bfloat16
AF = mybir.ActivationFunctionType
ALU = mybir.AluOpType
AX = mybir.AxisListType

D = 2048
DC = 16
KEYS = 4096
TOK = 2048
NQT = 16
NKB = 16
KB = 256
CKV = 512
CC = 4
NH = 16
NPAIR = 8
ROPE = 64
VD = 64
FF = 1408
NF = 11
NPASS = 10
SCL = 1.0 / math.sqrt(192.0)
EPS = 1e-5
BASE = 10000.0

ENGINES = ("tensor", "vector", "scalar", "sync", "gpsimd")


def _bf(x):
    return np.ascontiguousarray(np.asarray(x).astype(ml_dtypes.bfloat16))


def _f32(x):
    return np.ascontiguousarray(np.asarray(x, dtype=np.float32))


def _lhsT_chunks(WT, m):
    K = WT.shape[0]
    return _f32(WT.reshape(K // 128, 128, m).transpose(1, 0, 2).reshape(128, (K // 128) * m))


def prep_shared(inp):
    d = {}
    Wdkv = _f32(inp["L0_attn_W_down_kv"])
    Wdq = _f32(inp["L0_attn_W_down_q"])
    Wkr = _f32(inp["L0_attn_W_kr"])
    Wupk = _f32(inp["L0_attn_W_up_k"])
    Wupv = _f32(inp["L0_attn_W_up_v"])
    Wupq = _f32(inp["L0_attn_W_up_q"])
    Wqr = _f32(inp["L0_attn_W_qr"])
    Wfc = _f32(inp["L0_attn_W_fc"])

    d["wA_dkv"] = _lhsT_chunks(Wdkv.T, 512)
    d["wA_dq"] = _lhsT_chunks(Wdq.T, 512)
    d["wA_kr"] = _lhsT_chunks(Wkr.T, 64)
    d["wA_upk"] = _lhsT_chunks(Wupk.T, 1024)
    d["wA_upv"] = _lhsT_chunks(Wupv.T, 1024)
    qcqr = np.empty((CKV, NH * 128), dtype=np.float32)
    for h in range(NH):
        qcqr[:, h * 128:h * 128 + 64] = Wupq.T[:, h * 64:(h + 1) * 64]
        qcqr[:, h * 128 + 64:h * 128 + 128] = Wqr.T[:, h * 64:(h + 1) * 64]
    d["wA_qcqr"] = _lhsT_chunks(qcqr, 2048)
    d["wA_fc"] = _lhsT_chunks(Wfc.T, 2048)

    d["b_dkv"] = _f32(np.asarray(inp["L0_attn_b_down_kv"]).reshape(4, 128).T)
    d["b_dq"] = _f32(np.asarray(inp["L0_attn_b_down_q"]).reshape(4, 128).T)
    d["b_kr"] = _f32(np.asarray(inp["L0_attn_b_kr"]).reshape(64, 1))
    d["b_upk"] = _f32(np.tile(np.asarray(inp["L0_attn_b_up_k"]).reshape(16, 64).T, (2, 1)))
    bq = np.asarray(inp["L0_attn_b_up_q"]).reshape(16, 64)
    br = np.asarray(inp["L0_attn_b_qr"]).reshape(16, 64)
    d["b_qcqr"] = _f32(np.concatenate([bq.T, br.T], axis=0))
    d["b_upv"] = _f32(np.asarray(inp["L0_attn_b_up_v"]).reshape(1, 1024))
    d["b_fc"] = _f32(np.asarray(inp["L0_attn_b_fc"]).reshape(1, 2048))

    i2 = np.arange(0, ROPE, 2, dtype=np.float64)
    theta = 1.0 / BASE ** (i2 / ROPE)
    ang = np.arange(KEYS, dtype=np.float64)[:, None] * theta[None, :]
    d["_sinr"] = _f32(np.repeat(np.sin(ang), 2, axis=1).T)
    d["_cosr"] = _f32(np.repeat(np.cos(ang), 2, axis=1).T)
    d["Tk_sin"] = d["_sinr"]
    d["Tk_cos"] = d["_cosr"]

    Rr = np.zeros((64, 64), dtype=np.float32)
    for i in range(32):
        Rr[2 * i, 2 * i + 1] = -1.0
        Rr[2 * i + 1, 2 * i] = 1.0
    Rq = np.zeros((128, 128), dtype=np.float32)
    Rq[64:, 64:] = Rr
    d["RqT"] = _bf(Rq.T)
    d["R64T"] = _bf(Rr.T)
    d["I128"] = _bf(np.eye(128))
    d["I128f"] = _f32(np.eye(128))

    d["wR"] = _lhsT_chunks(_f32(inp["L0_ffn_w_router"]), 8)

    w1 = _f32(inp["L0_ffn_w1"])
    w2 = _f32(inp["L0_ffn_w2"])
    w3 = _f32(inp["L0_ffn_w3"])
    ws1 = _f32(inp["L0_ffn_ws1"])
    ws2 = _f32(inp["L0_ffn_ws2"])
    ws3 = _f32(inp["L0_ffn_ws3"])
    up_all = np.concatenate([w1, ws1.reshape(D, 2, FF).transpose(1, 0, 2)], axis=0)
    gate_all = np.concatenate([w3, ws3.reshape(D, 2, FF).transpose(1, 0, 2)], axis=0)
    down_all = np.concatenate([w2, ws2.reshape(2, FF, D)], axis=0)

    def w13(a):
        return _f32(a.reshape(NPASS, DC, 128, NF, 128).transpose(0, 3, 2, 1, 4)
                    .reshape(NPASS, NF, 128, 2048))

    d["W1L"] = w13(up_all)
    d["W3L"] = w13(gate_all)
    d["W2L"] = _f32(down_all.reshape(NPASS, NF, 128, 2048).transpose(0, 2, 1, 3)
                    .reshape(NPASS, 128, NF * 2048))
    return d


def prep_core(inp, shared, core):
    b, half = core // 2, core % 2
    h = _f32(inp["L0_attn_h"][b])
    d = {k: v for k, v in shared.items() if not k.startswith("_")}
    d["h_kb"] = _f32(h.reshape(NKB, KB, DC, 128).transpose(0, 3, 2, 1).reshape(NKB, 128, DC * KB))
    ho = h[half * TOK:(half + 1) * TOK]
    d["h_q_kb"] = _f32(ho.reshape(8, KB, DC, 128).transpose(0, 3, 2, 1).reshape(8, 128, DC * KB))
    d["h_own"] = _f32(ho.reshape(NQT, 128, D))
    tq_s = np.zeros((128, TOK), dtype=np.float32)
    tq_c = np.zeros((128, TOK), dtype=np.float32)
    tq_s[64:] = shared["_sinr"][:, half * TOK:(half + 1) * TOK]
    tq_c[64:] = shared["_cosr"][:, half * TOK:(half + 1) * TOK]
    d["Tq_sin"] = _f32(tq_s)
    d["Tq_cos"] = _f32(tq_c)
    return d


INPUT_SPECS = {
    "h_kb": (F32, [NKB, 128, DC * KB]),
    "h_q_kb": (F32, [8, 128, DC * KB]),
    "h_own": (F32, [NQT, 128, D]),
    "wA_dkv": (F32, [128, 8192]), "wA_dq": (F32, [128, 8192]), "wA_kr": (F32, [128, 1024]),
    "wA_upk": (F32, [128, 4096]), "wA_upv": (F32, [128, 4096]), "wA_qcqr": (F32, [128, 8192]),
    "wA_fc": (F32, [128, 16384]),
    "b_dkv": (F32, [128, 4]), "b_dq": (F32, [128, 4]), "b_kr": (F32, [64, 1]),
    "b_upk": (F32, [128, 16]), "b_qcqr": (F32, [128, 16]), "b_upv": (F32, [1, 1024]),
    "b_fc": (F32, [1, 2048]),
    "Tk_sin": (F32, [64, 4096]), "Tk_cos": (F32, [64, 4096]),
    "Tq_sin": (F32, [128, 2048]), "Tq_cos": (F32, [128, 2048]),
    "RqT": (BF16, [128, 128]), "R64T": (BF16, [64, 64]),
    "I128": (BF16, [128, 128]), "I128f": (F32, [128, 128]),
    "wR": (F32, [128, 128]),
    "W1L": (F32, [NPASS, NF, 128, 2048]), "W3L": (F32, [NPASS, NF, 128, 2048]),
    "W2L": (F32, [NPASS, 128, NF * 2048]),
}


class Build:
    def __init__(self, nc):
        self.nc = nc
        self.q = {e: [] for e in ENGINES}
        self.cnt = {}
        self.sems = {}
        self.semnames = []

    def sem(self, name):
        if name not in self.cnt:
            self.cnt[name] = 0
            self.semnames.append(name)
        return name

    def emit(self, eng, fn):
        self.q[eng].append(fn)

    def wait(self, eng, sem, val):
        if val is None or val <= 0:
            return
        self.sem(sem)
        self.emit(eng, lambda e, s=sem, v=val: e.wait_ge(self.sems[s], v))

    def op(self, eng, fn, inc=None, n=1):
        if inc is None:
            self.emit(eng, fn)
            return None
        self.sem(inc)
        self.cnt[inc] += n
        val = self.cnt[inc]
        self.emit(eng, lambda e, f=fn, s=inc, k=n: f(e).then_inc(self.sems[s], k))
        return val

    def finalize(self):
        nc = self.nc
        import contextlib
        with contextlib.ExitStack() as stack:
            for name in self.semnames:
                self.sems[name] = stack.enter_context(nc.semaphore(name))
            with nc.Block() as block:
                for eng in ENGINES:
                    getattr(block, eng)(lambda e, q=self.q[eng]: [f(e) for f in q])
        return nc


class Ring:

    def __init__(self, B, slots, producer="tensor"):
        self.B = B
        self.slots = slots
        self.free_at = [("v", 0)] * len(slots)
        self.i = 0
        self.producer = producer

    def next(self):
        idx = self.i % len(self.slots)
        self.i += 1
        sem, val = self.free_at[idx]
        self.B.wait(self.producer, sem, val)
        return idx

    def ap(self, idx):
        s = self.slots[idx]
        return s() if callable(s) else s

    def drained(self, idx, sem, val):
        self.free_at[idx] = (sem, val)


A3_SUB = 9
NPASS_RUN = NPASS


def build_program(debug=(), phases=6):
    nc = bass.Bass("TRN2", target_bir_lowering=False)
    B = Build(nc)

    inp = {}
    for name, (dt, shape) in INPUT_SPECS.items():
        inp[name] = nc.dram_tensor(name, shape, dt, kind="ExternalInput").ap()

    out_hbm = nc.dram_tensor("out", [TOK, D], F32, kind="ExternalOutput").ap()
    x1_hbm = nc.dram_tensor("x1_hbm", [TOK, D], F32,
                            kind="ExternalOutput" if "x1" in debug else "Internal").ap()
    y_hbm = nc.dram_tensor("y_hbm", [TOK, D], F32,
                           kind="ExternalOutput" if "y" in debug else "Internal").ap()

    def dbg(name, shape, dt=F32):
        return nc.dram_tensor("dbg_" + name, shape, dt, kind="ExternalOutput").ap() \
            if name in debug else None

    dbg_ckv = dbg("ckv", [128, CC * KEYS])
    dbg_cq = dbg("cq", [128, CC * TOK])
    dbg_krr = dbg("krr", [128, KEYS])
    dbg_K = dbg("K", [2, 128, KEYS])
    dbg_V = dbg("V", [128, 32 * 130])
    dbg_Q = dbg("Q", [2, 128, TOK])
    dbg_OT = dbg("OT", [128, 8 * TOK])
    dbg_cmb = dbg("cmb", [128, 128])
    dbg_z = dbg("z", [128, 2048])
    dbg_bfcB = dbg_z and nc.dram_tensor("dbg_bfcB", [128, 2048], F32, kind="ExternalOutput").ap()
    dbg_stats = dbg_z and nc.dram_tensor("dbg_stats", [128, 16], F32, kind="ExternalOutput").ap()
    dbg_h1s = dbg("h1s", [128, NF * 2048])

    A = nc.alloc_sbuf_tensor
    I128 = A("sb_I128", [128, 128], BF16)
    I128f = A("sb_I128f", [128, 128], F32)
    RqT = A("sb_RqT", [128, 128], BF16)
    R64T = A("sb_R64T", [64, 64], BF16)
    ones1f = A("ones1f", [1, 128], F32)
    b_dkv = A("sb_b_dkv", [128, 4], F32)
    b_dq = A("sb_b_dq", [128, 4], F32)
    b_kr = A("sb_b_kr", [64, 1], F32)
    b_upk = A("sb_b_upk", [128, 16], F32)
    b_qcqr = A("sb_b_qcqr", [128, 16], F32)
    smll = A("smll", [128, 16], F32)
    cmb_all = A("cmb_all", [128, 128], F32)
    eps_col = A("eps_col", [128, 1], F32)

    ARB1 = A("ARB1", [128, 24576], BF16)
    ARB2 = A("ARB2", [128, 55296], BF16)
    ARF = A("ARF", [128, 8192], F32)
    ARF2 = A("ARF2", [128, 4096], F32)

    ckv_sb = ARB1[:, 0:16384]
    cq_sb = ARB1[:, 16384:24576]
    h1s_sb = ARB1[:, 0:22528]
    ARB1f = ARB1.bitcast(F32)
    hob = [ARB1f[:, 0:2048], ARB1f[:, 2048:4096]]
    bfcB = ARB1f[:, 4096:6144]
    b_fc_sb = ARB1f[0:1, 6144:8192]
    outb = [ARB1f[:, 0:2048], ARB1f[:, 2048:4096]]

    S3, S4, S5, S6 = 0, 16384, 22528, 30720
    S7, S8 = 34944, 51328
    wA_dkv_sb = ARB2[:, S3:S3 + 8192]
    wA_dq_sb = ARB2[:, S3 + 8192:S3 + 16384]
    wA_kr_sb = ARB2[:, S4:S4 + 1024]
    krw_sb = ARB2[:, S4 + 1024:S4 + 5120]
    wA_upk_sb = ARB2[:, S3:S3 + 4096]
    wA_upv_sb = ARB2[:, S3 + 4096:S3 + 8192]
    wA_qcqr_sb = ARB2[:, S3 + 8192:S3 + 16384]
    wA_fc_sb = ARB2[:, S3:S3 + 16384]
    Qh = [ARB2[:, S4:S4 + 2048], ARB2[:, S4 + 2048:S4 + 4096]]
    Pr = ARB2[:, S4 + 4096:S4 + 6144]
    Kb = [ARB2[:, S5:S5 + 4096], ARB2[:, S5 + 4096:S5 + 8192]]
    Vp = ARB2[:, S6:S6 + 4224]
    OT_sb = ARB2[:, S7:S7 + 16384]
    W2s_sb = ARB2[:, S3:S3 + 22528]
    xT_sb = ARB2[:, S5:S5 + 32768]
    x1bf_sb = ARB2[:, S4:S4 + 2048]
    x1bf_b = ARB2[:, S4 + 2048:S4 + 4096]
    x1bf_c = ARB2[:, S4 + 4096:S4 + 6144]

    hb = [ARF[:, 0:2048].bitcast(BF16), ARF[:, 2048:4096].bitcast(BF16)]
    Tq_sin = ARF[:, 0:2048]
    Tq_cos = ARF[:, 2048:4096]
    Tk_sin = ARF[0:64, 4096:6144]
    Tk_cos = ARF[0:64, 6144:8192]
    krsc = [ARF[0:64, 0:512], ARF[0:64, 512:1024]]
    lb_sb = ARF2[0:64, 0:512]
    lrow = [ARF2[0:1, 2048:2560], ARF2[0:1, 2560:3072]]
    lrow_ln = [ARF2[0:1, 3072:3584], ARF2[0:1, 3584:4096]]
    rsc = ARF2[:, 512:1024]
    bvt = ARF2[:, 1024:1152]
    b_upv_sb = ARF[0:1, 1024:2048]
    z_sb = ARF[:, 0:2048]
    x1f = [ARF[:, 2048:4096], ARF[:, 4096:6144]]
    sigT = ARF[0:8, 0:2048]
    stok = ARF[:, 2048:2176]
    cmbt = ARF[:, 2176:2304]
    top8 = ARF[:, 2304:2312]
    cmb_sc = ARF[:, 2312:2440]
    ystage = [ARF[:, i * 1024:(i + 1) * 1024] for i in range(2)]
    tsil = ARB1[:, 22528:23552].bitcast(F32)
    wR_sb = ARB1[:, 23552:23680]
    w1ring = [ARF2[:, 0:1024].bitcast(BF16), ARF2[:, 1024:2048].bitcast(BF16)]
    w3ring = [ARF2[:, 2048:3072].bitcast(BF16), ARF2[:, 3072:4096].bitcast(BF16)]
    x1b = [ARF[:, 0:2048], ARF[:, 2048:4096]]
    yb = [ARF[:, 4096:6144], ARF[:, 6144:8192]]

    PA = nc.alloc_psum_tensor("PA", [128, 1024], F32)
    PB = nc.alloc_psum_tensor("PB", [128, 1024], F32)
    PC = nc.alloc_psum_tensor("PC", [128, 1024], F32)
    PE = nc.alloc_psum_tensor("PE", [128, 512], F32)
    PF = nc.alloc_psum_tensor("PF", [128, 512], F32)
    PEb = PE.bitcast(BF16)

    def vop(fn):
        vv = B.op("vector", fn, inc="v")
        B.wait("vector", "v", vv)
        return vv

    def g_dma(dst, src, accum=False):
        if accum:
            return B.op("gpsimd", lambda e, d=dst, s=src:
                        e.dma_start(out=d, in_=s, accum_op=ALU.add), inc="g", n=16)
        return B.op("gpsimd", lambda e, d=dst, s=src: e.dma_start(out=d, in_=s), inc="g", n=16)

    def s_dma(dst, src):
        return B.op("sync", lambda e, d=dst, s=src: e.dma_start(out=d, in_=s), inc="hw", n=16)

    B.op("vector", lambda e: e.memset(ones1f[:], 1.0), inc="v")
    B.op("vector", lambda e: e.memset(eps_col[:], EPS), inc="v")
    v_vones = B.op("vector", lambda e: e.memset(Vp, 1.0), inc="v")
    hw0 = 0
    for dst, src in [(I128[:], inp["I128"]), (I128f[:], inp["I128f"]), (RqT[:], inp["RqT"]),
                     (R64T[:], inp["R64T"]), (b_dkv[:], inp["b_dkv"]), (b_dq[:], inp["b_dq"]),
                     (b_kr[:], inp["b_kr"]), (b_upk[:], inp["b_upk"]),
                     (b_qcqr[:], inp["b_qcqr"])]:
        hw0 = s_dma(dst, src)
    g_w1 = 0
    for dst, src in [(wA_dkv_sb, inp["wA_dkv"]), (wA_dq_sb, inp["wA_dq"]),
                     (wA_kr_sb, inp["wA_kr"])]:
        g_w1 = g_dma(dst, src)

    a1_slots = [lambda: PA[:, 0:256], lambda: PA[:, 512:768],
                lambda: PB[:, 0:256], lambda: PB[:, 512:768],
                lambda: PC[:, 0:256], lambda: PC[:, 512:768]]
    a1r = Ring(B, a1_slots)

    hwh = [g_dma(hb[i], inp["h_kb"][i]) for i in range(2)]
    t_hb = [0, 0]

    def a1_group(rhs_sb, w_sb, w_stride, m_off, mpart, bias, dst):
        si = a1r.next()
        ps = a1r.ap(si)
        tv = None
        for c in range(DC):
            def mm(e, ps=ps, c=c, rhs=rhs_sb, w=w_sb, ws=w_stride, mo=m_off, mp=mpart,
                   first=(c == 0), last=(c == DC - 1)):
                return e.matmul(ps[0:mp, :], w[:, c * ws + mo:c * ws + mo + mp],
                                rhs[:, c * 256:(c + 1) * 256], start=first, stop=last)
            tv = B.op("tensor", mm, inc="t" if c == DC - 1 else None)
        if si % 2 == 0:
            B.wait("vector", "t", tv)
            vv = B.op("vector", lambda e, ps=ps, mp=mpart, b=bias, d=dst:
                      e.tensor_scalar(d, ps[0:mp, :], b, None, ALU.add), inc="v")
            a1r.drained(si, "v", vv)
        else:
            B.wait("scalar", "t", tv)
            vv = B.op("scalar", lambda e, ps=ps, mp=mpart, b=bias, d=dst:
                      e.activation(d, ps[0:mp, :], AF.Identity, bias=b), inc="s")
            a1r.drained(si, "s", vv)
        return tv

    B.wait("tensor", "g", g_w1)
    B.wait("tensor", "hw", hw0)
    for kb in range(NKB):
        B.wait("tensor", "g", hwh[kb])
        rhs = hb[kb % 2]
        for ci in range(CC):
            t_hb[kb % 2] = a1_group(rhs, wA_dkv_sb, 512, ci * 128, 128, b_dkv[:, ci:ci + 1],
                                    ckv_sb[:, ci * KEYS + kb * 256: ci * KEYS + kb * 256 + 256])
        t_hb[kb % 2] = a1_group(rhs, wA_kr_sb, 64, 0, 64, b_kr[:],
                                krw_sb[0:64, kb * 256:kb * 256 + 256])
        if kb + 2 < NKB:
            B.wait("gpsimd", "t", t_hb[kb % 2])
            hwh.append(g_dma(hb[kb % 2], inp["h_kb"][kb + 2]))
    hwq = []
    for i in range(2):
        B.wait("gpsimd", "t", t_hb[i])
        hwq.append(g_dma(hb[i], inp["h_q_kb"][i]))
    t_hbq = [0, 0]
    for kb in range(8):
        B.wait("tensor", "g", hwq[kb])
        rhs = hb[kb % 2]
        for ci in range(CC):
            t_hbq[kb % 2] = a1_group(rhs, wA_dq_sb, 512, ci * 128, 128, b_dq[:, ci:ci + 1],
                                     cq_sb[:, ci * TOK + kb * 256: ci * TOK + kb * 256 + 256])
        if kb + 2 < 8:
            B.wait("gpsimd", "t", t_hbq[kb % 2])
            hwq.append(g_dma(hb[kb % 2], inp["h_q_kb"][kb + 2]))
    t_a1_done = B.cnt["t"]
    v_a1_done = B.cnt["v"]

    if dbg_ckv is not None:
        B.wait("gpsimd", "v", v_a1_done)
        g_dma(dbg_ckv, ckv_sb)
        g_dma(dbg_cq, cq_sb)

    if phases <= 1:
        B.wait("sync", "v", B.cnt["v"])
        B.wait("sync", "hw", B.cnt["hw"])
        B.wait("gpsimd", "g", B.cnt["g"])
        return B.finalize()

    B.wait("gpsimd", "t", t_a1_done)
    g_up = 0
    for dst, src in [(wA_upk_sb, inp["wA_upk"]), (wA_upv_sb, inp["wA_upv"]),
                     (wA_qcqr_sb, inp["wA_qcqr"])]:
        g_up = g_dma(dst, src)
    B.wait("sync", "t", t_a1_done)
    B.wait("sync", "v", v_a1_done)
    hw_tab = 0
    for dst, src in [(Tq_sin, inp["Tq_sin"]), (Tq_cos, inp["Tq_cos"]),
                     (b_upv_sb, inp["b_upv"])]:
        hw_tab = s_dma(dst, src)

    t_rot_drain = 0
    v_tk_read = 0
    for khalf in range(2):
        B.wait("sync", "v", v_tk_read)
        hw_tk = s_dma(Tk_sin, inp["Tk_sin"][:, khalf * 2048:(khalf + 1) * 2048])
        hw_tk = s_dma(Tk_cos, inp["Tk_cos"][:, khalf * 2048:(khalf + 1) * 2048])
        for nth in range(4):
            nt = khalf * 4 + nth
            B.wait("tensor", "v", t_rot_drain)
            tv = B.op("tensor", lambda e, nt=nt: e.matmul(
                PE[0:64, 0:512], R64T[:], krw_sb[0:64, nt * 512:(nt + 1) * 512],
                start=True, stop=True), inc="t")
            B.wait("vector", "t", tv)
            if nth == 0:
                B.wait("vector", "hw", hw_tk)
            B.op("vector", lambda e, nt=nt, nth=nth: e.tensor_tensor(
                krsc[0], krw_sb[0:64, nt * 512:(nt + 1) * 512],
                Tk_sin[:, nth * 512:(nth + 1) * 512], ALU.mult))
            vop(lambda e, nth=nth: e.tensor_tensor(
                krsc[1], PE[0:64, 0:512],
                Tk_cos[:, nth * 512:(nth + 1) * 512], ALU.mult))
            B.op("vector", lambda e, nt=nt: e.tensor_tensor(
                Kb[0][64:128, nt * 512:(nt + 1) * 512], krsc[0],
                krsc[1], ALU.add))
            vv = B.op("vector", lambda e, nt=nt: e.tensor_copy(
                Kb[1][64:128, nt * 512:(nt + 1) * 512],
                Kb[0][64:128, nt * 512:(nt + 1) * 512]), inc="v")
            t_rot_drain = vv
        v_tk_read = t_rot_drain
    v_krr = B.cnt["v"]
    if dbg_krr is not None:
        B.wait("gpsimd", "v", v_krr)
        g_dma(dbg_krr, Kb[0])

    tK = [0, 0]
    tQ = [0, 0]
    tV = 0
    vOT = 0
    sLB = 0
    tLB = 0
    tP = [0, 0]
    ait = [0]
    obanks = [PE, PF]
    o_free = [("v", 0), ("v", 0)]
    v_lastevac = [0]
    lrow_tmm = [0, 0]
    sPA = [0]
    pending_lb = [None]

    s_ring = Ring(B, [PB, PC])
    qb_slots = [lambda: PA[:, 0:512], lambda: PA[:, 512:1024],
                lambda: PB[:, 0:512], lambda: PB[:, 512:1024],
                lambda: PC[:, 0:512], lambda: PC[:, 512:1024]]
    qbr = Ring(B, qb_slots)

    B.wait("tensor", "g", g_up)
    B.wait("tensor", "v", v_krr)
    B.wait("tensor", "v", v_vones)
    B.wait("tensor", "hw", hw_tab)

    def build_group(w_sb, w_stride, w_off, mpart, rhs_sb, rhs_stride, rhs_off, rhs_n):
        si = qbr.next()
        ps = qbr.ap(si)
        tv = None
        for c in range(CC):
            def mm(e, ps=ps, c=c, w=w_sb, ws=w_stride, wo=w_off, mp=mpart,
                   r=rhs_sb, rs=rhs_stride, ro=rhs_off, n=rhs_n,
                   first=(c == 0), last=(c == CC - 1)):
                return e.matmul(ps[0:mp, 0:n], w[:, c * ws + wo:c * ws + wo + mp],
                                r[:, c * rs + ro:c * rs + ro + n], start=first, stop=last)
            tv = B.op("tensor", mm, inc="t" if c == CC - 1 else None)
        return si, ps, tv

    for p in range(NPAIR):
        h0, h1 = 2 * p, 2 * p + 1
        if pending_lb[0] is not None:
            pending_lb[0]()
            pending_lb[0] = None
        B.wait("tensor", "s", sPA[0])
        for _slot in range(2):
            _sem, _val = s_ring.free_at[_slot]
            B.wait("tensor", _sem, _val)
        for nt in range(8):
            si, ps, tv = build_group(wA_upk_sb, 1024, p * 128, 128, ckv_sb, KEYS, nt * 512, 512)
            B.wait("scalar", "t", tv)
            B.wait("scalar", "t", tK[0])
            B.op("scalar", lambda e, ps=ps, nt=nt, p=p: e.activation(
                Kb[0][0:64, nt * 512:(nt + 1) * 512], ps[0:64, :],
                AF.Identity, bias=b_upk[0:64, 2 * p:2 * p + 1]))
            B.wait("scalar", "t", tK[1])
            vv = B.op("scalar", lambda e, ps=ps, nt=nt, p=p: e.activation(
                Kb[1][0:64, nt * 512:(nt + 1) * 512], ps[64:128, :],
                AF.Identity, bias=b_upk[64:128, 2 * p + 1:2 * p + 2]), inc="s")
            qbr.drained(si, "s", vv)
        vK = ("s", B.cnt["s"])

        sem, val = o_free[1]
        B.wait("tensor", sem, val)
        tv = B.op("tensor", lambda e, p=p: e.matmul(
            PF[:, 0:128], ones1f[:], b_upv_sb[:, p * 128:(p + 1) * 128],
            start=True, stop=True), inc="t")
        B.wait("scalar", "t", tv)
        sbvt = B.op("scalar", lambda e: e.copy(bvt[:], PF[:, 0:128]), inc="s")
        o_free[1] = ("s", sbvt)
        for tc in range(32):
            si, ps, tv = build_group(ckv_sb, KEYS, tc * 128, 128, wA_upv_sb, 1024, p * 128, 128)
            B.wait("vector", "t", tv)
            if tc == 0:
                B.wait("vector", "t", tV)
                B.wait("vector", "s", sbvt)
            B.op("vector", lambda e, ps=ps, tc=tc: e.tensor_tensor(
                Vp[:, tc * 130:tc * 130 + 64], ps[:, 0:64], bvt[:, 0:64], ALU.add))
            vv = B.op("vector", lambda e, ps=ps, tc=tc: e.tensor_tensor(
                Vp[:, tc * 130 + 65:tc * 130 + 129], ps[:, 64:128], bvt[:, 64:128],
                ALU.add), inc="v")
            qbr.drained(si, "v", vv)
        vV = B.cnt["v"]

        vQ = [0, 0]
        for hi, h in enumerate((h0, h1)):
            for nt in range(4):
                si, ps, tv = build_group(wA_qcqr_sb, 2048, h * 128, 128, cq_sb, TOK, nt * 512, 512)
                B.wait("scalar", "t", tv)
                B.wait("scalar", "t", tQ[hi])
                vv = B.op("scalar", lambda e, ps=ps, nt=nt, h=h, hi=hi: e.activation(
                    Qh[hi][:, nt * 512:(nt + 1) * 512], ps[:],
                    AF.Identity, bias=b_qcqr[:, h:h + 1]), inc="s")
                qbr.drained(si, "s", vv)
            v_qraw = ("s", B.cnt["s"])
            for nt in range(4):
                B.wait("tensor", v_qraw[0], v_qraw[1])
                sem, val = o_free[0]
                B.wait("tensor", sem, val)
                tv = B.op("tensor", lambda e, hi=hi, nt=nt: e.matmul(
                    PE[:, 0:512], RqT[:], Qh[hi][:, nt * 512:(nt + 1) * 512],
                    start=True, stop=True), inc="t")
                B.op("vector", lambda e, hi=hi, nt=nt: e.tensor_tensor(
                    rsc[64:128, 0:512], Qh[hi][64:128, nt * 512:(nt + 1) * 512],
                    Tq_sin[64:128, nt * 512:(nt + 1) * 512], ALU.mult))
                B.wait("vector", "t", tv)
                vop(lambda e, hi=hi, nt=nt: e.tensor_tensor(
                    Qh[hi][64:128, nt * 512:(nt + 1) * 512], PE[64:128, 0:512],
                    Tq_cos[64:128, nt * 512:(nt + 1) * 512], ALU.mult))
                vv = B.op("vector", lambda e, hi=hi, nt=nt: e.tensor_tensor(
                    Qh[hi][64:128, nt * 512:(nt + 1) * 512],
                    Qh[hi][64:128, nt * 512:(nt + 1) * 512],
                    rsc[64:128, 0:512], ALU.add), inc="v")
                o_free[0] = ("v", vv)
            vQ[hi] = B.cnt["v"]

        if p == 0 and dbg_K is not None:
            B.wait("gpsimd", "v", B.cnt["v"])
            g_dma(dbg_K[0], Kb[0])
            g_dma(dbg_K[1], Kb[1])
            g_dma(dbg_V, Vp)
            g_dma(dbg_Q[0], Qh[0])
            g_dma(dbg_Q[1], Qh[1])

        for hi, h in enumerate((h0, h1)):
            for qt in range(4):
                obank = obanks[ait[0] % 2]
                t_pv_last = None

                def emit_s(cg, hi=hi, qt=qt):
                    si = s_ring.next()
                    ps = s_ring.ap(si)
                    tv = None
                    for j in range(2):
                        kcix = 2 * cg + j
                        def smm(e, ps=ps, j=j, kcix=kcix, hi=hi, qt=qt):
                            return e.matmul(ps[:, j * 512:(j + 1) * 512],
                                            Kb[hi][:, kcix * 128:(kcix + 1) * 128],
                                            Qh[hi][:, qt * 512:(qt + 1) * 512],
                                            start=True, stop=True)
                        tv = B.op("tensor", smm, inc="t" if j == 1 else None)
                    return si, ps, tv

                B.wait("tensor", vK[0], vK[1])
                B.wait("tensor", "v", vQ[hi])
                B.wait("tensor", "v", vV)
                s_pend = emit_s(0)
                for cg in range(16):
                    si, ps, tv = s_pend
                    tK[hi] = tv
                    tQ[hi] = tv
                    slot = cg % 2
                    B.wait("scalar", "t", tv)
                    B.wait("scalar", "t", tP[slot])
                    sv = B.op("scalar", lambda e, ps=ps, slot=slot: e.activation(
                        Pr[:, slot * 1024:(slot + 1) * 1024], ps[:], AF.Exp,
                        bias=0.0, scale=SCL), inc="s")
                    s_ring.drained(si, "s", sv)
                    if cg + 1 < 16:
                        s_pend = emit_s(cg + 1)
                    if cg == 0 and pending_lb[0] is not None:
                        pending_lb[0]()
                        pending_lb[0] = None
                    B.wait("tensor", "s", sv)
                    if cg == 0:
                        sem, val = o_free[ait[0] % 2]
                        B.wait("tensor", sem, val)
                    for j in range(2):
                        kcix = 2 * cg + j
                        def pvmm(e, j=j, kcix=kcix, hi=hi, slot=slot, ob=obank,
                                 first=(cg == 0 and j == 0), last=(cg == 15 and j == 1)):
                            return e.matmul(
                                ob[:, 0:512],
                                Vp[:, kcix * 130 + hi * 65:kcix * 130 + hi * 65 + 128],
                                Pr[:, slot * 1024 + j * 512:slot * 1024 + (j + 1) * 512],
                                start=first, stop=last)
                        t_pv_last = B.op("tensor", pvmm, inc="t" if j == 1 else None)
                    tP[slot] = t_pv_last
                    tV = t_pv_last
                lslot = ait[0] % 2
                B.wait("vector", "t", t_pv_last)
                B.wait("vector", "t", lrow_tmm[lslot])
                vl = B.op("vector", lambda e, ob=obank, lslot=lslot: e.tensor_copy(
                    lrow[lslot], ob[64:65, 0:512]), inc="v")
                B.wait("scalar", "v", vl)
                s_ln = B.op("scalar", lambda e, lslot=lslot: e.activation(
                    lrow_ln[lslot], lrow[lslot], AF.Ln), inc="s")

                def make_lb(s_ln=s_ln, lslot=lslot, ob=obank, h=h, qt=qt, oslot=ait[0] % 2):
                    def emit():
                        B.wait("tensor", "s", s_ln)
                        B.wait("tensor", "s", sPA[0])
                        tlb = B.op("tensor", lambda e: e.matmul(
                            PA[0:64, 0:512], ones1f[:, 0:64], lrow_ln[lslot],
                            start=True, stop=True), inc="t")
                        lrow_tmm[lslot] = tlb
                        B.wait("scalar", "t", tlb)
                        B.wait("scalar", "v", v_lastevac[0])
                        slb = B.op("scalar", lambda e: e.activation(
                            lb_sb, PA[0:64, 0:512], AF.Exp, scale=-1.0), inc="s")
                        sPA[0] = slb
                        B.wait("vector", "s", slb)
                        oc = h // 2
                        r0 = (h % 2) * 64
                        vev = B.op("vector", lambda e, oc=oc, r0=r0, qt=qt, ob=ob:
                                   e.tensor_tensor(
                                       OT_sb[r0:r0 + 64,
                                             oc * 2048 + qt * 512:oc * 2048 + (qt + 1) * 512],
                                       ob[0:64, 0:512], lb_sb, ALU.mult), inc="v")
                        o_free[oslot] = ("v", vev)
                        v_lastevac[0] = vev
                    return emit
                pending_lb[0] = make_lb()
                ait[0] += 1
    if pending_lb[0] is not None:
        pending_lb[0]()
        pending_lb[0] = None
    v_a2_done = B.cnt["v"]
    t_a2_done = B.cnt["t"]
    s_a2_done = B.cnt["s"]
    if dbg_OT is not None:
        B.wait("gpsimd", "v", v_a2_done)
        g_dma(dbg_OT, OT_sb)

    if phases <= 2:
        B.wait("sync", "v", B.cnt["v"])
        B.wait("sync", "hw", B.cnt["hw"])
        B.wait("gpsimd", "g", B.cnt["g"])
        return B.finalize()

    B.wait("gpsimd", "t", t_a2_done)
    g_fc = g_dma(wA_fc_sb, inp["wA_fc"])
    B.wait("sync", "t", t_a2_done)
    B.wait("sync", "v", v_a2_done)
    hw_bfc = s_dma(b_fc_sb, inp["b_fc"])
    B.wait("tensor", "hw", hw_bfc)
    B.wait("tensor", "v", v_a2_done)
    B.wait("tensor", "s", sPA[0])
    sLB = 0
    for nt in range(4):
        t_b = B.op("tensor", lambda e, nt=nt: e.matmul(
            PF[:, 0:512], ones1f[:], b_fc_sb[:, nt * 512:(nt + 1) * 512],
            start=True, stop=True), inc="t")
        B.wait("scalar", "t", t_b)
        sLB = B.op("scalar", lambda e, nt=nt: e.copy(bfcB[:, nt * 512:(nt + 1) * 512],
                                                     PF[:, 0:512]), inc="s")
        B.wait("tensor", "s", sLB)
    s_bfc = B.cnt["s"]

    hwho = [s_dma(hob[i], inp["h_own"][i]) for i in range(2)]
    pending_tr = [None]
    str_free = [0, 0, 0]
    PFb = PF.bitcast(BF16)
    tr_out = [PEb, PFb]
    fc_ring = Ring(B, [PB, PC])
    fc_ring.free_at = [("s", s_a2_done), ("s", s_a2_done)]
    v_tr_drain = [0, 0, 0, 0]
    hw_x1 = [0, 0]
    v_hob = [0, 0]

    B.wait("tensor", "g", g_fc)
    B.wait("tensor", "s", s_bfc)
    B.wait("tensor", "v", v_a2_done)
    B.wait("scalar", "t", t_a2_done)
    for qt in range(NQT if A3_SUB > 0 else 0):
        tg = []
        for g in range(2):
            si = fc_ring.next()
            ps = fc_ring.ap(si)
            tv = None
            for sub in range(2):
                nt = 2 * g + sub
                if g == 1 and sub == 0 and pending_tr[0] is not None:
                    pending_tr[0]()
                    pending_tr[0] = None
                for oc in range(8):
                    def fmm(e, ps=ps, sub=sub, nt=nt, oc=oc, qt=qt,
                            first=(oc == 0), last=(oc == 7)):
                        return e.matmul(
                            ps[:, sub * 512:(sub + 1) * 512],
                            OT_sb[:, oc * 2048 + qt * 128:oc * 2048 + qt * 128 + 128],
                            wA_fc_sb[:, oc * 2048 + nt * 512:oc * 2048 + (nt + 1) * 512],
                            start=first, stop=last)
                    tv = B.op("tensor", fmm, inc="t" if oc == 7 else None)
            tg.append((si, ps, tv))
        if A3_SUB < 2:
            for g, (si, ps, tv) in enumerate(tg):
                B.wait("vector", "t", tv)
                vv = B.op("vector", lambda e, ps=ps: e.tensor_copy(
                    z_sb[:, 0:1024], ps[:]), inc="v")
                fc_ring.drained(si, "v", vv)
            continue
        B.wait("vector", "hw", hwho[qt])
        vv = None
        for g, (si, ps, tv) in enumerate(tg):
            B.wait("vector", "t", tv)
            vop(lambda e, ps=ps, g=g: e.scalar_tensor_tensor(
                z_sb[:, g * 1024:(g + 1) * 1024], ps[:], 1.0,
                bfcB[:, g * 1024:(g + 1) * 1024], ALU.mult, ALU.add))
            vv = B.op("vector", lambda e, g=g, qt=qt: e.tensor_tensor(
                z_sb[:, g * 1024:(g + 1) * 1024], z_sb[:, g * 1024:(g + 1) * 1024],
                hob[qt % 2][:, g * 1024:(g + 1) * 1024], ALU.add), inc="v")
            fc_ring.drained(si, "v", vv)
        v_hob[qt % 2] = vv
        v_z = vv
        if qt + 2 < NQT:
            B.wait("sync", "v", v_hob[qt % 2])
            hwho.append(s_dma(hob[qt % 2], inp["h_own"][qt + 2]))
        if A3_SUB < 3:
            continue
        B.wait("scalar", "v", v_z)
        B.wait("scalar", "hw", hw_x1[qt % 2])
        sq = B.op("scalar", lambda e, qt=qt: e.activation(
            x1f[qt % 2][:], z_sb[:], AF.Square, accum_out=smll[:, 0:1]), inc="s")
        vop(lambda e: e.reduce_sum(smll[:, 1:2], z_sb[:], axis=AX.X))
        B.wait("vector", "s", sq)
        vop(lambda e: e.tensor_scalar(smll[:, 2:3], smll[:, 1:2], 1.0 / D, None, ALU.mult))
        vop(lambda e: e.tensor_scalar(smll[:, 3:4], smll[:, 0:1], 1.0 / D, None, ALU.mult))
        vop(lambda e: e.tensor_tensor(smll[:, 4:5], smll[:, 2:3], smll[:, 2:3], ALU.mult))
        vv = B.op("vector", lambda e: e.tensor_tensor(smll[:, 5:6], smll[:, 3:4], smll[:, 4:5], ALU.subtract), inc="v")
        B.wait("vector", "v", vv)
        B.wait("scalar", "v", vv)
        sv = B.op("scalar", lambda e: e.activation(smll[:, 6:7], smll[:, 5:6], AF.Sqrt, bias=eps_col[:]), inc="s")
        B.wait("vector", "s", sv)
        vop(lambda e: e.reciprocal(smll[:, 7:8], smll[:, 6:7]))
        vop(lambda e: e.tensor_scalar(smll[:, 8:9], smll[:, 2:3], smll[:, 7:8], -1.0, ALU.mult, ALU.mult))
        vx1 = B.op("vector", lambda e, qt=qt: e.tensor_scalar(
            x1f[qt % 2][:], z_sb[:], smll[:, 7:8], smll[:, 8:9], ALU.mult, ALU.add), inc="v")
        if qt == 0 and dbg_z is not None:
            B.wait("sync", "v", vx1)
            s_dma(dbg_z, z_sb[:])
            s_dma(dbg_bfcB, hob[qt % 2][:])
            s_dma(dbg_stats, smll[:])
            dmpv = s_dma(nc.dram_tensor("dbg_bfc2", [128, 2048], F32, kind="ExternalOutput").ap(), bfcB[:])
            B.wait("vector", "hw", dmpv)
            B.wait("scalar", "hw", dmpv)
            B.wait("sync", "hw", dmpv)
        if A3_SUB < 4:
            continue
        B.wait("sync", "v", vx1)
        hw_x1[qt % 2] = s_dma(x1_hbm[qt * 128:(qt + 1) * 128, :], x1f[qt % 2])
        if A3_SUB < 5:
            continue
        B.wait("scalar", "v", vx1)
        stage = x1bf_sb if qt == 0 else (x1bf_b if qt % 2 else x1bf_c)
        B.wait("scalar", "t", str_free[0 if qt == 0 else (1 if qt % 2 else 2)])
        scast = B.op("scalar", lambda e, qt=qt, st=stage: e.copy(st, x1f[qt % 2][:]), inc="s")
        if qt == 0:
            scast_q0 = scast
            continue

        def make_tr(scast=scast, qt=qt, stage=stage, sslot=(1 if qt % 2 else 2)):
            def emit():
                for c in range(DC if A3_SUB >= 6 else 0):
                    trslot = c % 2
                    if c == 0:
                        B.wait("tensor", "s", scast)
                        B.wait("tensor", "s", sLB if qt == 1 else 0)
                    B.wait("tensor", "v", v_tr_drain[trslot])
                    tv = B.op("tensor", lambda e, c=c, trslot=trslot, stage=stage:
                              e.transpose(tr_out[trslot][:, 0:128],
                                          stage[:, c * 128:(c + 1) * 128], I128[:]),
                              inc="t")
                    str_free[sslot] = tv
                    B.wait("vector", "t", tv)
                    vv = B.op("vector", lambda e, c=c, qt=qt, trslot=trslot:
                              e.tensor_copy(
                                  xT_sb[:, c * 2048 + qt * 128:c * 2048 + qt * 128 + 128],
                                  tr_out[trslot][:, 0:128]), inc="v")
                    v_tr_drain[trslot] = vv
            return emit
        pending_tr[0] = make_tr()
    if pending_tr[0] is not None:
        pending_tr[0]()
        pending_tr[0] = None
    if A3_SUB >= 6:
        for c in range(DC):
            trslot = c % 2
            if c == 0:
                B.wait("tensor", "s", scast_q0)
            B.wait("tensor", "v", v_tr_drain[trslot])
            tv = B.op("tensor", lambda e, c=c, trslot=trslot: e.transpose(
                tr_out[trslot][:, 0:128], x1bf_sb[:, c * 128:(c + 1) * 128],
                I128[:]), inc="t")
            B.wait("vector", "t", tv)
            vv = B.op("vector", lambda e, c=c, trslot=trslot: e.tensor_copy(
                xT_sb[:, c * 2048 + 0 * 128:c * 2048 + 0 * 128 + 128],
                tr_out[trslot][:, 0:128]), inc="v")
            v_tr_drain[trslot] = vv
    v_a3_done = B.cnt["v"]
    t_a3_done = B.cnt["t"]
    hw_a3_done = B.cnt["hw"]

    if phases <= 3:
        B.wait("sync", "v", B.cnt["v"])
        B.wait("sync", "hw", B.cnt["hw"])
        B.wait("gpsimd", "g", B.cnt["g"])
        return B.finalize()

    B.wait("gpsimd", "t", t_a3_done)
    g_wr = g_dma(wR_sb, inp["wR"])
    B.wait("tensor", "g", g_wr)
    B.wait("tensor", "v", v_a3_done)
    s_sig = 0
    for nt in range(4):
        tv = None
        B.wait("tensor", "s", s_sig)
        for c in range(DC):
            def rmm(e, c=c, nt=nt, first=(c == 0), last=(c == DC - 1)):
                return e.matmul(PF[0:8, 0:512], wR_sb[:, c * 8:(c + 1) * 8],
                                xT_sb[:, c * 2048 + nt * 512:c * 2048 + (nt + 1) * 512],
                                start=first, stop=last)
            tv = B.op("tensor", rmm, inc="t" if c == DC - 1 else None)
        B.wait("scalar", "t", tv)
        B.wait("scalar", "hw", hw_a3_done)
        s_sig = B.op("scalar", lambda e, nt=nt: e.activation(
            sigT[:, nt * 512:(nt + 1) * 512], PF[0:8, 0:512], AF.Sigmoid), inc="s")
    v_cm = 0
    t_cmt = 0
    for qt in range(NQT):
        B.wait("tensor", "s", s_sig)
        B.wait("tensor", "v", v_cm)
        tv = B.op("tensor", lambda e, qt=qt: e.transpose(
            PE[:, 0:8], sigT[:, qt * 128:(qt + 1) * 128], I128f[0:8, 0:8]), inc="t")
        B.wait("vector", "t", tv)
        B.wait("vector", "hw", hw_a3_done)
        vop(lambda e: e.tensor_copy(stok[:, 0:8], PE[:, 0:8]))
        vop(lambda e: e.max(out=top8[:], in_=stok[:, 0:8]))
        vop(lambda e: e.tensor_tensor(smll[:, 0:1], top8[:, 0:1], top8[:, 1:2], ALU.add))
        vop(lambda e: e.reciprocal(smll[:, 1:2], smll[:, 0:1]))
        vop(lambda e: e.tensor_scalar(smll[:, 2:3], top8[:, 0:1], smll[:, 1:2], None, ALU.mult))
        vop(lambda e: e.tensor_scalar(smll[:, 3:4], top8[:, 1:2], smll[:, 1:2], None, ALU.mult))
        vop(lambda e: e.tensor_scalar(cmb_sc[:, 0:8], stok[:, 0:8], top8[:, 0:1], smll[:, 2:3], ALU.is_equal, ALU.mult))
        vop(lambda e: e.tensor_scalar(cmb_sc[:, 8:16], stok[:, 0:8], top8[:, 1:2], smll[:, 3:4], ALU.is_equal, ALU.mult))
        v_cm = B.op("vector", lambda e, qt=qt: e.tensor_tensor(
            cmb_all[:, qt * 8:(qt + 1) * 8], cmb_sc[:, 0:8], cmb_sc[:, 8:16], ALU.add),
            inc="v")
    v_f0_done = B.cnt["v"]
    if dbg_cmb is not None:
        B.wait("sync", "v", v_f0_done)
        s_dma(dbg_cmb, cmb_all[:])

    if phases <= 4:
        B.wait("sync", "v", B.cnt["v"])
        B.wait("sync", "hw", B.cnt["hw"])
        B.wait("gpsimd", "g", B.cnt["g"])
        return B.finalize()

    g_w1r = [0, 0]
    g_w3r = [0, 0]
    g_w2 = 0
    t_h1s_read = 0
    s_ystage = [0, 0]
    g_ystage = [0, 0]

    B.wait("gpsimd", "t", t_a3_done)
    g_w2 = g_dma(W2s_sb, inp["W2L"][0])
    for r in range(2):
        g_w1r[r] = g_dma(w1ring[r], inp["W1L"][0][r])
        g_w3r[r] = g_dma(w3ring[r], inp["W3L"][0][r])

    f1_slots = [(PA[:, 0:512], PA[:, 512:1024]),
                (PB[:, 0:512], PB[:, 512:1024]),
                (PE[:, 0:512], PF[:, 0:512])]
    f1_ring = Ring(B, f1_slots)
    f1_ring.free_at = [("v", v_a2_done), ("v", v_a3_done), ("v", v_f0_done)]
    y_ring = Ring(B, [PC, PA])
    y_ring.free_at = [("v", v_a3_done), ("v", 0)]

    for p in range(NPASS_RUN):
        for f in range(NF):
            r = (p * NF + f) % 2
            B.wait("tensor", "g", g_w1r[r])
            B.wait("tensor", "g", g_w3r[r])
            for tt in range(4):
                si = f1_ring.next()
                psA, psB = f1_ring.ap(si)
                tv = None
                for c in range(DC):
                    def m1(e, ps=psA, c=c, r=r, tt=tt, first=(c == 0), last=(c == DC - 1)):
                        return e.matmul(ps, w1ring[r][:, c * 128:(c + 1) * 128],
                                        xT_sb[:, c * 2048 + tt * 512:c * 2048 + (tt + 1) * 512],
                                        start=first, stop=last)
                    B.op("tensor", m1)
                for c in range(DC):
                    def m3(e, ps=psB, c=c, r=r, tt=tt, first=(c == 0), last=(c == DC - 1)):
                        return e.matmul(ps, w3ring[r][:, c * 128:(c + 1) * 128],
                                        xT_sb[:, c * 2048 + tt * 512:c * 2048 + (tt + 1) * 512],
                                        start=first, stop=last)
                    tv = B.op("tensor", m3, inc="t" if c == DC - 1 else None)
                B.wait("scalar", "t", tv)
                sv = B.op("scalar", lambda e, ps=psA: e.activation(
                    tsil[:], ps, AF.Silu), inc="s")
                B.wait("vector", "s", sv)
                if p > 0:
                    B.wait("vector", "t", t_h1s_read)
                vv = B.op("vector", lambda e, ps=psB, f=f, tt=tt: e.tensor_tensor(
                    h1s_sb[:, f * 2048 + tt * 512:f * 2048 + (tt + 1) * 512],
                    tsil[:], ps, ALU.mult), inc="v")
                f1_ring.drained(si, "v", vv)
            nf_, np_ = f + 2, p
            if nf_ >= NF:
                nf_ -= NF
                np_ += 1
            if np_ < NPASS:
                t_used = B.cnt["t"]
                B.wait("gpsimd", "t", t_used)
                g_w1r[r] = g_dma(w1ring[r], inp["W1L"][np_][nf_])
                g_w3r[r] = g_dma(w3ring[r], inp["W3L"][np_][nf_])
        v_f1 = B.cnt["v"]
        if p == 0 and dbg_h1s is not None:
            B.wait("gpsimd", "v", v_f1)
            g_dma(dbg_h1s, h1s_sb)

        B.wait("tensor", "g", g_w2)
        B.wait("tensor", "v", v_f1)
        y_ring.free_at[1] = f1_ring.free_at[0]
        for tt in range(NQT):
            for g2 in range(2):
                si = y_ring.next()
                ps = y_ring.ap(si)
                tv = None
                for sub in range(2):
                    nt = 2 * g2 + sub
                    for f in range(NF):
                        def ymm(e, ps=ps, sub=sub, nt=nt, f=f, tt=tt,
                                first=(f == 0), last=(f == NF - 1)):
                            return e.matmul(
                                ps[:, sub * 512:(sub + 1) * 512],
                                h1s_sb[:, f * 2048 + tt * 128:f * 2048 + tt * 128 + 128],
                                W2s_sb[:, f * 2048 + nt * 512:f * 2048 + (nt + 1) * 512],
                                start=first, stop=last)
                        tv = B.op("tensor", ymm, inc="t" if f == NF - 1 else None)
                yslot = (tt * 2 + g2) % 2
                if p < 8:
                    B.wait("vector", "t", tv)
                    B.wait("vector", "g", g_ystage[yslot])
                    if p == 0:
                        B.wait("vector", "v", v_f0_done)
                    sv = B.op("vector", lambda e, ps=ps, yslot=yslot, tt=tt, p=p:
                              e.tensor_scalar(ystage[yslot][:], ps[:],
                                              cmb_all[:, tt * 8 + p:tt * 8 + p + 1],
                                              None, ALU.mult), inc="v")
                    y_ring.drained(si, "v", sv)
                    B.wait("gpsimd", "v", sv)
                else:
                    B.wait("scalar", "t", tv)
                    B.wait("scalar", "g", g_ystage[yslot])
                    sv = B.op("scalar", lambda e, ps=ps, yslot=yslot: e.copy(
                        ystage[yslot][:], ps[:]), inc="s")
                    y_ring.drained(si, "s", sv)
                    B.wait("gpsimd", "s", sv)
                if si == 1:
                    f1_ring.free_at[0] = y_ring.free_at[si]
                s_ystage[yslot] = sv
                g_ystage[yslot] = g_dma(
                    y_hbm[tt * 128:(tt + 1) * 128, g2 * 1024:(g2 + 1) * 1024],
                    ystage[yslot], accum=(p > 0))
        t_h1s_read = B.cnt["t"]
        if p + 1 < NPASS:
            B.wait("gpsimd", "t", t_h1s_read)
            g_w2 = g_dma(W2s_sb, inp["W2L"][p + 1])
    g_y_done = B.cnt["g"]
    s_f2_done = B.cnt["s"]

    if phases <= 5:
        B.wait("sync", "v", B.cnt["v"])
        B.wait("sync", "hw", B.cnt["hw"])
        B.wait("gpsimd", "g", B.cnt["g"])
        return B.finalize()

    B.wait("sync", "g", g_y_done)
    B.wait("sync", "s", s_f2_done)
    hwx = []
    hwy = []
    for i in range(2):
        hwx.append(s_dma(x1b[i], x1_hbm[i * 128:(i + 1) * 128, :]))
        hwy.append(s_dma(yb[i], y_hbm[i * 128:(i + 1) * 128, :]))
    hw_ob = [0, 0]
    for qt in range(NQT):
        sl = qt % 2
        B.wait("vector", "hw", hwx[qt])
        B.wait("vector", "hw", hwy[qt])
        B.wait("vector", "t", t_h1s_read)
        vz = B.op("vector", lambda e, sl=sl: e.tensor_tensor(
            x1b[sl][:], x1b[sl][:], yb[sl][:], ALU.add), inc="v")
        B.wait("vector", "v", vz)
        B.wait("scalar", "v", vz)
        B.wait("scalar", "hw", hw_ob[sl])
        sq = B.op("scalar", lambda e, sl=sl: e.activation(
            outb[sl][:], x1b[sl][:], AF.Square, accum_out=smll[:, 0:1]), inc="s")
        vop(lambda e, sl=sl: e.reduce_sum(smll[:, 1:2], x1b[sl][:], axis=AX.X))
        B.wait("vector", "s", sq)
        vop(lambda e: e.tensor_scalar(smll[:, 2:3], smll[:, 1:2], 1.0 / D, None, ALU.mult))
        vop(lambda e: e.tensor_scalar(smll[:, 3:4], smll[:, 0:1], 1.0 / D, None, ALU.mult))
        vop(lambda e: e.tensor_tensor(smll[:, 4:5], smll[:, 2:3], smll[:, 2:3], ALU.mult))
        vv = B.op("vector", lambda e: e.tensor_tensor(smll[:, 5:6], smll[:, 3:4], smll[:, 4:5], ALU.subtract), inc="v")
        B.wait("vector", "v", vv)
        B.wait("scalar", "v", vv)
        sv = B.op("scalar", lambda e: e.activation(smll[:, 6:7], smll[:, 5:6], AF.Sqrt, bias=eps_col[:]), inc="s")
        B.wait("vector", "s", sv)
        vop(lambda e: e.reciprocal(smll[:, 7:8], smll[:, 6:7]))
        vop(lambda e: e.tensor_scalar(smll[:, 8:9], smll[:, 2:3], smll[:, 7:8], -1.0, ALU.mult, ALU.mult))
        vo = B.op("vector", lambda e, sl=sl: e.tensor_scalar(
            outb[sl][:], x1b[sl][:], smll[:, 7:8], smll[:, 8:9], ALU.mult, ALU.add), inc="v")
        B.wait("sync", "v", vo)
        hw_ob[sl] = s_dma(out_hbm[qt * 128:(qt + 1) * 128, :], outb[sl])
        if qt + 2 < NQT:
            B.wait("sync", "v", vo)
            hwx.append(s_dma(x1b[sl], x1_hbm[(qt + 2) * 128:(qt + 3) * 128, :]))
            hwy.append(s_dma(yb[sl], y_hbm[(qt + 2) * 128:(qt + 3) * 128, :]))
    B.wait("sync", "hw", B.cnt["hw"])
    B.wait("gpsimd", "g", B.cnt["g"])
    B.wait("vector", "hw", B.cnt["hw"])

    return B.finalize()


_prog_cache = {}


def get_program(debug=()):
    key = tuple(sorted(debug))
    if key not in _prog_cache:
        _prog_cache[key] = build_program(debug=key)
    return _prog_cache[key]


def kernel(**inputs):
    shared = prep_shared(inputs)
    in_maps = [prep_core(inputs, shared, c) for c in range(8)]
    nc = get_program()
    res = run_bass_kernel_spmd(nc, in_maps, core_ids=list(range(8)))
    out = np.empty((4, 4096, 2048), dtype=np.float32)
    for c in range(8):
        b, half = c // 2, c % 2
        out[b, half * TOK:(half + 1) * TOK, :] = res.results[c]["out"]
    return out
